# version 31
# baseline (speedup 1.0000x reference)
import contextlib
from collections import defaultdict

import numpy as np
import concourse.bass as bass
import concourse.mybir as mybir
from concourse.alu_op_type import AluOpType
from concourse.bass_utils import run_bass_kernel_spmd

F32 = mybir.dt.float32
BF16 = mybir.dt.bfloat16
AF = mybir.ActivationFunctionType
AX = mybir.AxisListType
AL = AluOpType

B, S, E, H = 4, 4096, 1024, 16
TQ = 2048
NCH = E // 128
KC = S // 128
INNER = 4096

MAGIC32 = 12582912.0
MAGICD = float(1.5 * 2**52 - 1.5 * 2**23)

DMA_INC = 16
ENGINES = ("tensor", "vector", "scalar", "gpsimd", "sync")


class Prog:

    NLANES = 8

    def __init__(self):
        self.ops = []
        self.last_writer = {}
        self.readers = defaultdict(list)
        self.fence_deps = []
        self.dma_count = defaultdict(int)

    def emit(self, engine, fn, reads=(), writes=(), dma=False, tag=""):
        idx = len(self.ops)
        deps = set(self.fence_deps)
        for r in reads:
            w = self.last_writer.get(r)
            if w is not None:
                deps.add(w)
        for r in writes:
            w = self.last_writer.get(r)
            if w is not None:
                deps.add(w)
            for rd in self.readers[r]:
                deps.add(rd)
        deps.discard(idx)
        if engine == "tensor" and not dma:
            deps = {d for d in deps
                    if not (self.ops[d]["engine"] == "tensor" and not self.ops[d]["dma"])}
        if dma:
            lane = f"{engine}_d{self.dma_count[engine] % self.NLANES}"
            self.dma_count[engine] += 1
        else:
            lane = engine
        self.ops.append(dict(idx=idx, engine=engine, lane=lane, fn=fn, dma=dma,
                             deps=deps, tag=tag, inc_needed=bool(dma), seq=None))
        for r in writes:
            self.last_writer[r] = idx
            self.readers[r] = []
        for r in reads:
            self.readers[r].append(idx)
        return idx

    def fence(self):
        last = {}
        for op in self.ops:
            last[op["lane"]] = op["idx"]
        self.fence_deps = list(last.values())

    def finalize(self):
        for op in self.ops:
            for d in op["deps"]:
                self.ops[d]["inc_needed"] = True
        counters = defaultdict(int)
        for op in self.ops:
            if op["inc_needed"]:
                counters[op["lane"]] += DMA_INC if op["dma"] else 1
                op["seq"] = counters[op["lane"]]
        self.lane_totals = dict(counters)
        return counters

    def emit_engine(self, name, eng, sems):
        watermark = defaultdict(int)
        lane_prev = defaultdict(int)
        for op in self.ops:
            if op["engine"] != name:
                continue
            need = {}
            for d in op["deps"]:
                dop = self.ops[d]
                if dop["seq"] is None:
                    continue
                ln = dop["lane"]
                if dop["seq"] > need.get(ln, 0):
                    need[ln] = dop["seq"]
            if op["dma"] and op["inc_needed"]:
                prev = lane_prev[op["lane"]]
                if prev > need.get(op["lane"], 0):
                    need[op["lane"]] = prev
            for ln, v in sorted(need.items()):
                if watermark[ln] < v:
                    eng.wait_ge(sems[ln], v)
                    watermark[ln] = v
            inst = op["fn"](eng)
            if op["inc_needed"]:
                if inst is None:
                    raise RuntimeError(f"op {op['idx']} ({op['tag']}) needs inc, got None")
                inst.then_inc(sems[op["lane"]], DMA_INC if op["dma"] else 1)
                lane_prev[op["lane"]] += DMA_INC if op["dma"] else 1


def build_program():
    nc = bass.Bass("TRN2", target_bir_lowering=False)

    d_xT = nc.dram_tensor("xT", [E, S], F32, kind="ExternalInput").ap()
    d_xo = nc.dram_tensor("xo", [TQ, E], F32, kind="ExternalInput").ap()
    d_cc = nc.dram_tensor("cc", [128, NCH], F32, kind="ExternalInput").ap()
    d_wmod = nc.dram_tensor("wmod", [E, 6 * E], F32, kind="ExternalInput").ap()
    d_wav = nc.dram_tensor("wav", [NCH, 128, 128], F32, kind="ExternalInput").ap()
    d_wvT = nc.dram_tensor("wvT", [E, E], F32, kind="ExternalInput").ap()
    d_wvb = nc.dram_tensor("wvb", [1, E], F32, kind="ExternalInput").ap()
    d_owT = nc.dram_tensor("owT", [E, E], F32, kind="ExternalInput").ap()
    d_owb = nc.dram_tensor("owb", [1, E], F32, kind="ExternalInput").ap()
    d_wgT = nc.dram_tensor("wgT", [E, INNER], F32, kind="ExternalInput").ap()
    d_wuT = nc.dram_tensor("wuT", [E, INNER], F32, kind="ExternalInput").ap()
    d_wdT = nc.dram_tensor("wdT", [INNER, E], F32, kind="ExternalInput").ap()
    d_ones = nc.dram_tensor("ones128", [128, 128], F32, kind="ExternalInput").ap()
    d_onesA = nc.dram_tensor("onesA", [128, 128], F32, kind="ExternalInput").ap()
    d_onesB = nc.dram_tensor("onesB", [128, 128], F32, kind="ExternalInput").ap()
    d_ident = nc.dram_tensor("ident128", [128, 128], F32, kind="ExternalInput").ap()
    d_y = nc.dram_tensor("y", [TQ, E], F32, kind="ExternalOutput").ap()

    d_h = [nc.dram_tensor(f"d_h{p}", [E, S], BF16).ap() for p in range(2)]
    d_q = [nc.dram_tensor(f"d_q{p}", [E, TQ], BF16).ap() for p in range(2)]
    d_v = [nc.dram_tensor(f"d_v{p}", [S, H, 65], BF16).ap() for p in range(2)]
    d_o = [nc.dram_tensor(f"d_o{p}", [E, TQ], BF16).ap() for p in range(2)]
    d_xp = nc.dram_tensor("d_xp", [TQ, E], F32).ap()
    d_wq = [nc.dram_tensor(n, [NCH if n != "d_wdq" else 32, 128, INNER if n != "d_wdq" else E],
                           BF16).ap() for n in ("d_wgq", "d_wuq", "d_wdq")]

    pr = Prog()
    stk = contextlib.ExitStack()

    def sb(name, shape, dtype=F32, st=None):
        return (st or stk).enter_context(nc.sbuf_tensor(name, shape, dtype))

    def psum(name, shape, dtype=F32, st=None):
        return (st or stk).enter_context(nc.psum_tensor(name, shape, dtype))

    def dve(fn, reads=(), writes=(), tag=""):
        return pr.emit("vector", fn, reads, writes, tag=tag)

    def gp(fn, reads=(), writes=(), tag=""):
        return pr.emit("gpsimd", fn, reads, writes, tag=tag)

    def sc(fn, reads=(), writes=(), tag=""):
        return pr.emit("scalar", fn, reads, writes, tag=tag)

    def pe(fn, reads=(), writes=(), tag=""):
        return pr.emit("tensor", fn, reads, writes, tag=tag)

    def dma(q, fn, reads=(), writes=(), tag=""):
        return pr.emit(q, fn, reads, writes, dma=True, tag=tag)

    with stk:
        ones = sb("ones", [128, 128])
        onesAb = sb("onesAb", [128, 128], BF16)
        onesBb = sb("onesBb", [128, 128], BF16)
        identb = sb("identb", [128, 128], BF16)
        magicC = sb("magicC", [128, 1])
        epscol = sb("epscol", [128, 1])
        ca1c = sb("ca1c", [128, NCH])
        sac = sb("sac", [128, NCH])
        ga_bc = sb("ga_bc", [128, E])
        gb_bc = sb("gb_bc", [128, E])
        cb1_bc = sb("cb1_bc", [128, E])
        sbb_bc = sb("sbb_bc", [128, E])
        owb_bc = sb("owb_bc", [128, E])
        wvb_bc = sb("wvb_bc", [128, E])
        negC = sb("negC", [128, H])
        negCp = sb("negCp", [128, H // 2])
        mqsq = sb("mqsq", [128, H])
        mksq = sb("mksq", [128, H])
        swq = sb("swq", [128, 3])

        dma("sync", lambda e: e.dma_start(out=ones[:], in_=d_ones[:]), writes=["ones"])
        p0s = contextlib.ExitStack()
        tmpA = sb("tmpA", [128, 128], st=p0s)
        dma("sync", lambda e: e.dma_start(out=tmpA[:], in_=d_onesA[:]), writes=["tA"])
        dve(lambda e: e.tensor_copy(onesAb[:], tmpA[:]), ["tA"], ["onesAb"])
        dma("sync", lambda e: e.dma_start(out=tmpA[:], in_=d_onesB[:]), writes=["tA"])
        dve(lambda e: e.tensor_copy(onesBb[:], tmpA[:]), ["tA"], ["onesBb"])
        dma("sync", lambda e: e.dma_start(out=tmpA[:], in_=d_ident[:]), writes=["tA"])
        dve(lambda e: e.tensor_copy(identb[:], tmpA[:]), ["tA"], ["identb"])
        dve(lambda e: e.tensor_scalar(magicC[:], ones[:, 0:1], 2.0 ** -40, 1.0,
                                      op0=AL.mult, op1=AL.add), ["ones"], ["mz0"])
        dve(lambda e: e.tensor_single_scalar(magicC[:], magicC[:], 1.0, op=AL.is_gt),
            ["mz0"], ["mz1"])
        dve(lambda e: e.tensor_scalar(magicC[:], magicC[:], MAGICD, MAGIC32,
                                      op0=AL.mult, op1=AL.add), ["mz1"], ["magicC"])
        dve(lambda e: e.tensor_scalar(epscol[:], ones[:, 0:1], 1e-5, 0.0,
                                      op0=AL.mult, op1=AL.add), ["ones"], ["epscol"])

        with contextlib.ExitStack() as p1:
            cc = sb("cc_sb", [128, NCH], st=p1)
            scol = sb("scol", [128, NCH], st=p1)
            wm = [sb(f"wm{i}", [128, 3072], st=p1) for i in range(2)]
            modrow = sb("modrow", [1, 6 * E], st=p1)
            mt1 = sb("mt1", [1, E], st=p1)
            mt2 = sb("mt2", [1, E], st=p1)
            dma("sync", lambda e: e.dma_start(
                out=cc[:], in_=d_cc[:]), writes=["cc"])
            sc(lambda e: e.activation(scol[:], cc[:], AF.Silu), ["cc"], ["scol"])
            with contextlib.ExitStack() as p1a:
                mps = psum("mps", [1, 3072], st=p1a)
                for half in range(2):
                    for c in range(NCH):
                        buf = wm[c % 2]
                        dma("sync", lambda e, c=c, half=half, buf=buf: e.dma_start(
                            out=buf[:], in_=d_wmod[c * 128:(c + 1) * 128,
                                                   half * 3072:(half + 1) * 3072]),
                            reads=[f"wmmm{c % 2}"], writes=[f"wm{c % 2}"],
                            tag=f"ldwmod{half}_{c}")

                        def modmm(e, c=c, buf=buf):
                            last = None
                            for nt in range(6):
                                last = e.matmul(mps[:, nt * 512:(nt + 1) * 512],
                                                scol[:, c:c + 1],
                                                buf[:, nt * 512:(nt + 1) * 512],
                                                start=(c == 0), stop=(c == NCH - 1))
                            return last
                        pe(modmm, ["scol", f"wm{c % 2}", "mps"],
                           ["mps", f"wmmm{c % 2}"], tag=f"modmm{half}_{c}")
                    dve(lambda e, half=half: e.tensor_copy(
                        modrow[:, half * 3072:(half + 1) * 3072], mps[:]),
                        ["mps"], ["modrow"], tag=f"modev{half}")
            pr.fence()
            with contextlib.ExitStack() as p1b:
                tps = psum("tps", [128, 128], st=p1b)
                idf = sb("idf", [128, 128], st=p1b)
                dma("sync", lambda e: e.dma_start(out=idf[:], in_=d_ident[:]), writes=["idf"])
                for j, dst, nm in ((0, sac, "sac"), (1, ca1c, "ca1c")):
                    for c in range(NCH):
                        pe(lambda e, j=j, c=c: e.transpose(
                            tps[:, 0:1], modrow[:, j * E + c * 128:j * E + (c + 1) * 128],
                            idf[0:1, 0:1]), ["modrow", "idf", nm], ["tpsr"],
                           tag=f"modT{j}_{c}")
                        dve(lambda e, dst=dst, c=c: e.tensor_copy(dst[:, c:c + 1],
                                                                  tps[:, 0:1]),
                            ["tpsr"], [nm], tag=f"modTev{j}_{c}")
                dve(lambda e: e.tensor_single_scalar(ca1c[:], ca1c[:], 1.0, op=AL.add),
                    ["ca1c"], ["ca1c"])
            pr.fence()
            with contextlib.ExitStack() as p1c:
                bps = psum("bps", [128, 1024], st=p1c)

                def bcast(srcap, dst, dstname, readset):
                    def mm(e, srcap=srcap):
                        last = None
                        for nt in range(2):
                            last = e.matmul(bps[:, nt * 512:(nt + 1) * 512], ones[0:1, :],
                                            srcap[:, nt * 512:(nt + 1) * 512],
                                            start=True, stop=True)
                        return last
                    pe(mm, list(readset) + ["ones", "bpsr"], ["bps"], tag=f"bc_{dstname}")
                    dve(lambda e, dst=dst: e.tensor_copy(dst[:], bps[:]),
                        ["bps"], [dstname, "bpsr"], tag=f"bcev_{dstname}")

                def clampg(off):
                    dve(lambda e, off=off: e.tensor_scalar(
                        mt1[:], modrow[:, off:off + E], 0.0, 2.0, op0=AL.is_ge,
                        op1=AL.mult), ["modrow", "mt1r"], ["mt1"], tag=f"cg1_{off}")
                    dve(lambda e: e.tensor_single_scalar(mt1[:], mt1[:], 1.0,
                                                         op=AL.subtract), ["mt1"], ["mt1"])
                    sc(lambda e, off=off: e.activation(mt2[:], modrow[:, off:off + E],
                                                       AF.Abs),
                       ["modrow", "mt2r"], ["mt2"], tag=f"cg2_{off}")
                    dve(lambda e: e.tensor_single_scalar(mt2[:], mt2[:], 0.25, op=AL.max),
                        ["mt2"], ["mt2"])
                    dve(lambda e: e.tensor_tensor(mt2[:], mt2[:], mt1[:], op=AL.mult),
                        ["mt2", "mt1"], ["mt2", "mt1r"])

                clampg(2 * E)
                bcast(mt2, ga_bc, "ga_bc", ["mt2"])
                pr.last_writer["mt2r"] = pr.ops[-1]["idx"]
                clampg(5 * E)
                bcast(mt2, gb_bc, "gb_bc", ["mt2"])
                pr.last_writer["mt2r"] = pr.ops[-1]["idx"]
                dve(lambda e: e.tensor_single_scalar(mt1[:], modrow[:, 4 * E:5 * E], 1.0,
                                                     op=AL.add),
                    ["modrow", "mt1r"], ["mt1"], tag="cb1")
                bcast(mt1, cb1_bc, "cb1_bc", ["mt1"])
                pr.last_writer["mt1r"] = pr.ops[-1]["idx"]
                bcast(modrow[:, 3 * E:4 * E], sbb_bc, "sbb_bc", ["modrow"])
                vbr = sb("vbr", [1, E], st=p1c)
                obr = sb("obr", [1, E], st=p1c)
                dma("sync", lambda e: e.dma_start(out=vbr[:], in_=d_wvb[:]), writes=["vbr"])
                dma("sync", lambda e: e.dma_start(out=obr[:], in_=d_owb[:]), writes=["obr"])
                bcast(vbr, wvb_bc, "wvb_bc", ["vbr"])
                bcast(obr, owb_bc, "owb_bc", ["obr"])
            pr.fence()
        p0s.close()

        qs = contextlib.ExitStack()
        qwb = [sb(f"qwb{i}", [128, 2048], st=qs) for i in range(2)]
        qpos = sb("qpos", [128, 2048], st=qs)
        qwq = sb("qwq", [128, 2048], BF16, st=qs)
        qacc = [sb(f"qacc{j}", [128, 64], st=qs) for j in range(3)]
        qtot = [sb(f"qtot{j}", [128, 1], st=qs) for j in range(3)]
        qrow = sb("qrow", [1, 512], st=qs)
        QSPECS = [(d_wgT, d_wq[0], NCH, INNER, 0, "d_wgq"),
                  (d_wuT, d_wq[1], NCH, INNER, 1, "d_wuq"),
                  (d_wdT, d_wq[2], 32, E, 2, "d_wdq")]

        def qpieces(d_w, nchunks, width):
            nsub = width // 2048 if width >= 2048 else 1
            subw = width // nsub
            for i in range(nchunks * nsub):
                c, si = divmod(i, nsub)
                yield i, c, si, subw, d_w[c * 128:(c + 1) * 128,
                                          si * subw:(si + 1) * subw]

        def quant_stageA(d_w, nchunks, width, j):
            for i, c, si, subw, piece in qpieces(d_w, nchunks, width):
                dma("gpsimd", lambda e, i=i, piece=piece, subw=subw: e.dma_start(
                    out=qwb[i % 2][:, 0:subw], in_=piece),
                    reads=[f"qwb{i % 2}"], writes=[f"qwb{i % 2}"], tag=f"qld{j}_{i}")
                dve(lambda e, i=i, j=j, subw=subw: e.reduce_sum(
                    qacc[j][:, i:i + 1], qwb[i % 2][:, 0:subw], axis=AX.X,
                    apply_absolute_value=True),
                    [f"qwb{i % 2}", f"qacc{j}"], [f"qacc{j}"], tag=f"qabs{j}_{i}")
                pr.last_writer[f"qwb{i % 2}"] = pr.ops[-1]["idx"]

        def quant_stageBC(d_w, d_wqt, nchunks, width, j, wqname):
            npc = sum(1 for _ in qpieces(d_w, nchunks, width))
            dve(lambda e, npc=npc, j=j: e.reduce_sum(qtot[j][:], qacc[j][:, 0:npc],
                                                     axis=AX.X),
                [f"qacc{j}", f"qtot{j}"], [f"qtot{j}"], tag=f"qtot{j}")
            dma("gpsimd", lambda e, j=j: e.dma_start(out=qrow[0:1, 0:128],
                                                     in_=qtot[j][:, 0:1]),
                reads=[f"qtot{j}", "qrow"], writes=["qrow"], tag=f"qgath{j}")
            dve(lambda e: e.reduce_sum(qrow[0:1, 128:129], qrow[0:1, 0:128], axis=AX.X),
                ["qrow"], ["qrow"], tag=f"qrsum{j}")
            dve(lambda e, j=j, nchunks=nchunks, width=width: e.tensor_single_scalar(
                qrow[0:1, 128:129], qrow[0:1, 128:129],
                1.0 / (nchunks * 128 * width), op=AL.mult), ["qrow"], ["qrow"])
            dve(lambda e: e.tensor_single_scalar(qrow[0:1, 128:129], qrow[0:1, 128:129],
                                                 1e-5, op=AL.max), ["qrow"], ["qrow"])
            dve(lambda e: e.tensor_scalar(qrow[0:1, 129:130], qrow[0:1, 128:129], -1.0,
                                          0.0, op0=AL.mult, op1=AL.add),
                ["qrow"], ["qrow"])
            dve(lambda e: e.reciprocal(qrow[0:1, 130:131], qrow[0:1, 128:129]),
                ["qrow"], ["qrow"])
            dve(lambda e: e.tensor_scalar(qrow[0:1, 131:259], ones[0:1, 0:128],
                                          qrow[0:1, 129:130], 0.0, op0=AL.mult,
                                          op1=AL.add), ["qrow", "ones"], ["qrow"],
                tag=f"qbc1_{j}")
            dve(lambda e: e.tensor_scalar(qrow[0:1, 259:387], ones[0:1, 0:128],
                                          qrow[0:1, 130:131], 0.0, op0=AL.mult,
                                          op1=AL.add), ["qrow", "ones"], ["qrow"],
                tag=f"qbc2_{j}")
            dma("gpsimd", lambda e, j=j: e.dma_start(out=swq[:, j:j + 1],
                                                     in_=qrow[0:1, 131:259]),
                reads=["qrow", "swq"], writes=["swq"], tag=f"qscat1_{j}")
            dma("gpsimd", lambda e, j=j: e.dma_start(out=qtot[j][:, 0:1],
                                                     in_=qrow[0:1, 259:387]),
                reads=["qrow", f"qtot{j}"], writes=[f"qtot{j}"], tag=f"qscat2_{j}")
            for i, c, si, subw, piece in qpieces(d_w, nchunks, width):
                dma("gpsimd", lambda e, i=i, piece=piece, subw=subw: e.dma_start(
                    out=qwb[i % 2][:, 0:subw], in_=piece),
                    reads=[f"qwb{i % 2}"], writes=[f"qwb{i % 2}"], tag=f"qld2{j}_{i}")
                dve(lambda e, i=i, j=j, subw=subw: e.tensor_scalar(
                    qwb[i % 2][:, 0:subw], qwb[i % 2][:, 0:subw], qtot[j][:], 0.0,
                    op0=AL.mult, op1=AL.add),
                    [f"qwb{i % 2}", f"qtot{j}"], [f"qwb{i % 2}"], tag=f"qmul{j}_{i}")
                dve(lambda e, i=i, subw=subw: e.tensor_single_scalar(
                    qpos[:, 0:subw], qwb[i % 2][:, 0:subw], 0.5, op=AL.is_gt),
                    [f"qwb{i % 2}", "qpos"], ["qpos"], tag=f"qpos{j}_{i}")
                dve(lambda e, i=i, subw=subw: e.scalar_tensor_tensor(
                    qwq[:, 0:subw], qwb[i % 2][:, 0:subw], -0.5, qpos[:, 0:subw],
                    op0=AL.is_lt, op1=AL.subtract),
                    [f"qwb{i % 2}", "qpos", "qwq"], ["qwq"], tag=f"qneg{j}_{i}")
                pr.last_writer[f"qwb{i % 2}"] = pr.ops[-1]["idx"]
                dma("gpsimd", lambda e, c=c, si=si, subw=subw, d_wqt=d_wqt: e.dma_start(
                    out=d_wqt[c, :, si * subw:(si + 1) * subw], in_=qwq[:, 0:subw]),
                    reads=["qwq"], writes=[wqname], tag=f"qst{j}_{c}_{si}")
                pr.last_writer["qwq"] = pr.ops[-1]["idx"]

        for (d_w, d_wqt, nchunks, width, j, wqname) in QSPECS:
            quant_stageA(d_w, nchunks, width, j)

        with contextlib.ExitStack() as p2:
            xc = [sb(f"xc{i}", [128, S], st=p2) for i in range(2)]
            xsq = sb("xsq", [128, 2048], st=p2)
            m_bc = sb("m_bc", [128, S], st=p2)
            rstd_bc = sb("rstd_bc", [128, S], st=p2)
            h32 = sb("h32", [128, S], st=p2)
            hhi = sb("hhi", [128, S], BF16, st=p2)
            hlo = sb("hlo", [128, S], BF16, st=p2)
            hsqb = sb("hsqb", [128, S], BF16, st=p2)
            wav32 = sb("wav32", [128, 128 * NCH], st=p2)
            wavp = [sb(f"wavp{p}", [128, 128 * NCH], BF16, st=p2) for p in range(2)]
            q32 = sb("q32", [128, TQ], st=p2)
            qhi = sb("qhi", [128, TQ], BF16, st=p2)
            qlo = sb("qlo", [128, TQ], BF16, st=p2)
            qsqb = sb("qsqb", [128, TQ], BF16, st=p2)

            for c in range(NCH):
                dma("sync", lambda e, c=c: e.dma_start(
                    out=wav32[:, c * 128:(c + 1) * 128], in_=d_wav[c, :, :]),
                    writes=["wav32"], tag=f"ldwav{c}")
            dve(lambda e: e.tensor_copy(wavp[0][:], wav32[:]), ["wav32"], ["wavh"])
            dve(lambda e: e.tensor_tensor(wavp[1][:], wav32[:], wavp[0][:],
                                          op=AL.subtract), ["wav32", "wavh"], ["wavl"])

            with contextlib.ExitStack() as p2a:
                st_ps = psum("st_ps", [128, 4096], st=p2a)
                for sh in range(2):
                    for c in range(NCH):
                        buf = xc[c % 2]
                        dma("sync", lambda e, sh=sh, c=c, buf=buf: e.dma_start(
                            out=buf[:, 0:2048],
                            in_=d_xT[c * 128:(c + 1) * 128, sh * 2048:(sh + 1) * 2048]),
                            reads=[f"xcr{c % 2}"], writes=[f"xc{c % 2}"],
                            tag=f"ldxs{sh}_{c}")
                        sc(lambda e, buf=buf: e.activation(xsq[:], buf[:, 0:2048],
                                                           AF.Square),
                           [f"xc{c % 2}", "xsqr"], ["xsq"], tag=f"sq{sh}_{c}")

                        def statmm(e, c=c, buf=buf):
                            last = None
                            for nt in range(4):
                                last = e.matmul(st_ps[:, nt * 512:(nt + 1) * 512], ones[:],
                                                buf[:, nt * 512:(nt + 1) * 512],
                                                start=(c == 0), stop=(c == NCH - 1))
                            for nt in range(4):
                                last = e.matmul(st_ps[:, 2048 + nt * 512:2048 + (nt + 1) * 512],
                                                ones[:], xsq[:, nt * 512:(nt + 1) * 512],
                                                start=(c == 0), stop=(c == NCH - 1))
                            return last
                        pe(statmm, [f"xc{c % 2}", "xsq", "ones", "st_ps"],
                           ["st_ps", f"xcr{c % 2}", "xsqr"], tag=f"stat{sh}_{c}")
                    sl = slice(sh * 2048, (sh + 1) * 2048)
                    dve(lambda e, sl=sl: e.tensor_single_scalar(
                        m_bc[:, sl], st_ps[:, 0:2048], 1.0 / E, op=AL.mult),
                        ["st_ps"], ["m_bc"], tag=f"mev{sh}")
                    dve(lambda e, sl=sl: e.tensor_tensor(
                        rstd_bc[:, sl], m_bc[:, sl], m_bc[:, sl], op=AL.mult),
                        ["m_bc"], ["m2"], tag=f"m2_{sh}")
                    dve(lambda e, sl=sl: e.scalar_tensor_tensor(
                        rstd_bc[:, sl], st_ps[:, 2048:4096], 1.0 / E, rstd_bc[:, sl],
                        op0=AL.mult, op1=AL.subtract),
                        ["st_ps", "m2"], ["var"], tag=f"var{sh}")
                    sc(lambda e, sl=sl: e.activation(rstd_bc[:, sl], rstd_bc[:, sl],
                                                     AF.Sqrt, bias=epscol[:]),
                       ["var", "epscol"], ["sstd"], tag=f"std{sh}")
                    dve(lambda e, sl=sl: e.reciprocal(rstd_bc[:, sl], rstd_bc[:, sl]),
                        ["sstd", "st_ps"], ["rstd_bc"], tag=f"rstd{sh}")
            pr.fence()

            with contextlib.ExitStack() as p2b:
                n_ps = psum("n_ps", [128, 2048], st=p2b)
                for c in range(NCH):
                    buf = xc[c % 2]
                    dma("sync", lambda e, c=c, buf=buf: e.dma_start(
                        out=buf[:], in_=d_xT[c * 128:(c + 1) * 128, :]),
                        reads=[f"xcr{c % 2}"], writes=[f"xc{c % 2}"], tag=f"ldxc{c}")
                    dve(lambda e, buf=buf: e.tensor_tensor(h32[:], buf[:], m_bc[:],
                                                           op=AL.subtract),
                        [f"xc{c % 2}", "m_bc", "h32r"], ["h32"], tag=f"h1_{c}")
                    pr.last_writer[f"xcr{c % 2}"] = pr.ops[-1]["idx"]
                    dve(lambda e: e.tensor_tensor(h32[:], h32[:], rstd_bc[:], op=AL.mult),
                        ["h32", "rstd_bc"], ["h32"])
                    dve(lambda e, c=c: e.tensor_scalar(h32[:], h32[:], ca1c[:, c:c + 1],
                                                       sac[:, c:c + 1], op0=AL.mult,
                                                       op1=AL.add),
                        ["h32", "ca1c", "sac"], ["h32"], tag=f"h3_{c}")
                    dve(lambda e: e.tensor_copy(hhi[:], h32[:]), ["h32", "hhir"], ["hhi"])
                    dve(lambda e: e.tensor_tensor(hlo[:], h32[:], hhi[:], op=AL.subtract),
                        ["h32", "hhi", "hlor"], ["hlo"])
                    dma("scalar", lambda e, c=c: e.dma_start(
                        out=d_h[0][c * 128:(c + 1) * 128, :], in_=hhi[:]),
                        reads=["hhi"], writes=["d_h0"], tag=f"sph0_{c}")
                    pr.last_writer["hhir"] = pr.ops[-1]["idx"]
                    dma("scalar", lambda e, c=c: e.dma_start(
                        out=d_h[1][c * 128:(c + 1) * 128, :], in_=hlo[:]),
                        reads=["hlo"], writes=["d_h1"], tag=f"sph1_{c}")
                    pr.last_writer["hlor"] = pr.ops[-1]["idx"]
                    sc(lambda e: e.activation(hsqb[:], h32[:], AF.Square),
                       ["h32", "hsqbr"], ["hsqb"], tag=f"hsq{c}")
                    for hh, lhs in ((0, onesAb), (1, onesBb)):
                        for rnd in range(2):
                            def knmm(e, lhs=lhs, rnd=rnd):
                                last = None
                                for nt in range(4):
                                    last = e.matmul(
                                        n_ps[:, nt * 512:(nt + 1) * 512], lhs[:],
                                        hsqb[:, rnd * 2048 + nt * 512:
                                             rnd * 2048 + (nt + 1) * 512],
                                        start=True, stop=True)
                                return last
                            pe(knmm, ["hsqb", "onesAb", "onesBb", "n_ps"], ["n_ps"],
                               tag=f"knmm{c}_{hh}_{rnd}")
                            tgt = mksq if rnd == 0 else xsq
                            tcol = 2 * c + hh if rnd == 0 else 0
                            dve(lambda e, tgt=tgt, tcol=tcol: e.reduce_max(
                                tgt[:, tcol:tcol + 1], n_ps[:], axis=AX.X),
                                ["n_ps", "xsq" if rnd == 1 else "mksq"],
                                [f"krm{c}_{hh}_{rnd}", "mksq" if rnd == 0 else "xsq"],
                                tag=f"krm{c}_{hh}_{rnd}")
                        dve(lambda e, c=c, hh=hh: e.tensor_tensor(
                            mksq[:, 2 * c + hh:2 * c + hh + 1],
                            mksq[:, 2 * c + hh:2 * c + hh + 1], xsq[:, 0:1], op=AL.max),
                            [f"krm{c}_{hh}_0", f"krm{c}_{hh}_1", "xsq"], ["mksq"],
                            tag=f"kcmb{c}_{hh}")
                    pr.last_writer["hsqbr"] = pr.ops[-1]["idx"]

                    def qmm(e, c=c, buf=buf):
                        last = None
                        for pi, (lp, rp) in enumerate(((0, 0), (0, 1), (1, 0))):
                            rt = hhi if rp == 0 else hlo
                            for nt in range(4):
                                last = e.matmul(
                                    n_ps[:, nt * 512:(nt + 1) * 512],
                                    wavp[lp][:, c * 128:(c + 1) * 128],
                                    rt[:, nt * 512:(nt + 1) * 512],
                                    start=(pi == 0), stop=(pi == 2))
                        return last
                    pe(qmm, ["hhi", "hlo", "wavh", "wavl", "n_ps"], ["n_ps"],
                       tag=f"qmm{c}")
                    dve(lambda e: e.tensor_copy(q32[:], n_ps[:]),
                        ["n_ps", "q32r"], ["q32"], tag=f"qev{c}")
                    dve(lambda e: e.tensor_copy(qhi[:], q32[:]), ["q32", "qhir"], ["qhi"])
                    dve(lambda e: e.tensor_tensor(qlo[:], q32[:], qhi[:], op=AL.subtract),
                        ["q32", "qhi", "qlor"], ["qlo"])
                    dma("scalar", lambda e, c=c: e.dma_start(
                        out=d_q[0][c * 128:(c + 1) * 128, :], in_=qhi[:]),
                        reads=["qhi"], writes=["d_q0"], tag=f"spq0_{c}")
                    pr.last_writer["qhir"] = pr.ops[-1]["idx"]
                    dma("scalar", lambda e, c=c: e.dma_start(
                        out=d_q[1][c * 128:(c + 1) * 128, :], in_=qlo[:]),
                        reads=["qlo"], writes=["d_q1"], tag=f"spq1_{c}")
                    pr.last_writer["qlor"] = pr.ops[-1]["idx"]
                    sc(lambda e: e.activation(qsqb[:], q32[:], AF.Square),
                       ["q32", "qsqbr"], ["qsqb"], tag=f"qsq{c}")
                    pr.last_writer["q32r"] = pr.ops[-1]["idx"]
                    for hh, lhs in ((0, onesAb), (1, onesBb)):
                        def qnmm(e, lhs=lhs):
                            last = None
                            for nt in range(4):
                                last = e.matmul(n_ps[:, nt * 512:(nt + 1) * 512], lhs[:],
                                                qsqb[:, nt * 512:(nt + 1) * 512],
                                                start=True, stop=True)
                            return last
                        pe(qnmm, ["qsqb", "onesAb", "onesBb", "n_ps"], ["n_ps"],
                           tag=f"qnmm{c}_{hh}")
                        dve(lambda e, c=c, hh=hh: e.reduce_max(
                            mqsq[:, 2 * c + hh:2 * c + hh + 1], n_ps[:], axis=AX.X),
                            ["n_ps", "mqsq"], ["mqsq", f"qrm{c}_{hh}"],
                            tag=f"qrm{c}_{hh}")
                    pr.last_writer["qsqbr"] = pr.ops[-1]["idx"]
            sc(lambda e: e.activation(mqsq[:], mqsq[:], AF.Sqrt), ["mqsq"], ["mq_s"])
            sc(lambda e: e.activation(mksq[:], mksq[:], AF.Sqrt), ["mksq"], ["mk_s"])
            dve(lambda e: e.tensor_tensor(negC[:], mqsq[:], mksq[:], op=AL.mult),
                ["mq_s", "mk_s"], ["negC0"])
            dve(lambda e: e.tensor_single_scalar(negC[:], negC[:], -0.1275, op=AL.mult),
                ["negC0"], ["negC"])
            for pp in range(H // 2):
                dve(lambda e, pp=pp: e.tensor_tensor(
                    negCp[:, pp:pp + 1], negC[:, 2 * pp:2 * pp + 1],
                    negC[:, 2 * pp + 1:2 * pp + 2], op=AL.min),
                    ["negC", "negCp"], ["negCp"], tag=f"negCp{pp}")
            pr.fence()

        with contextlib.ExitStack() as p3:
            wvp = [sb(f"wvp{p}", [128, NCH * E], BF16, st=p3) for p in range(2)]
            wv32 = sb("wv32", [128, E], st=p3)
            hsl = [[sb(f"hsl{p}_{i}", [128, NCH * 512], BF16, st=p3) for i in range(2)]
                   for p in range(2)]
            v32p = [sb(f"v32_{i}", [128, E], st=p3) for i in range(2)]
            vaug = [sb(f"vaug{p}", [128, H * 65], BF16, st=p3) for p in range(2)]
            vps = [psum(f"vps{i}", [128, 1024], st=p3) for i in range(2)]

            dve(lambda e: e.tensor_copy(
                vaug[0][:].rearrange("p (h w) -> p h w", w=65)[:, :, 64:65],
                ones[:, 0:16].rearrange("p (h one) -> p h one", one=1)),
                ["ones"], ["vaug0c"], tag="vaug_ones")
            dve(lambda e: e.tensor_scalar(
                vaug[1][:].rearrange("p (h w) -> p h w", w=65)[:, :, 64:65],
                ones[:, 0:16].rearrange("p (h one) -> p h one", one=1), 0.0, 0.0,
                op0=AL.mult, op1=AL.add), ["ones"], ["vaug1c"], tag="vaug_zeros")
            for c in range(NCH):
                dma("sync", lambda e, c=c: e.dma_start(
                    out=wv32[:], in_=d_wvT[c * 128:(c + 1) * 128, :]),
                    reads=["wv32r"], writes=["wv32"], tag=f"ldwv{c}")
                dve(lambda e, c=c: e.tensor_copy(wvp[0][:, c * E:(c + 1) * E], wv32[:]),
                    ["wv32"], ["wvph"], tag=f"wvh{c}")
                dve(lambda e, c=c: e.tensor_tensor(wvp[1][:, c * E:(c + 1) * E], wv32[:],
                                                   wvp[0][:, c * E:(c + 1) * E],
                                                   op=AL.subtract),
                    ["wv32", "wvph"], ["wvpl"], tag=f"wvl{c}")
                pr.last_writer["wv32r"] = pr.ops[-1]["idx"]
            for kc in range(KC):
                par = kc % 2
                gpar3 = (kc // 4) % 2
                if kc % 4 == 0:
                    for c in range(NCH):
                        for pl in range(2):
                            dma("sync", lambda e, kc=kc, c=c, pl=pl, gpar3=gpar3:
                                e.dma_start(
                                    out=hsl[pl][gpar3][:, c * 512:(c + 1) * 512],
                                    in_=d_h[pl][c * 128:(c + 1) * 128,
                                                kc * 128:(kc + 4) * 128]),
                                reads=[f"hsl{pl}{gpar3}", "d_h0", "d_h1"],
                                writes=[f"hsl{pl}{gpar3}"], tag=f"ldh{kc}_{c}_{pl}")
                koff = (kc % 4) * 128

                def vmm(e, gpar3=gpar3, koff=koff, par=par):
                    last = None
                    for oh in range(2):
                        for pi, (lp, rp) in enumerate(((0, 0), (0, 1), (1, 0))):
                            for c in range(NCH):
                                last = e.matmul(
                                    vps[par][:, oh * 512:(oh + 1) * 512],
                                    hsl[lp][gpar3][:, c * 512 + koff:c * 512 + koff + 128],
                                    wvp[rp][:, c * E + oh * 512:c * E + (oh + 1) * 512],
                                    start=(pi == 0 and c == 0),
                                    stop=(pi == 2 and c == NCH - 1))
                    return last
                pe(vmm, [f"hsl0{gpar3}", f"hsl1{gpar3}", "wvph", "wvpl", f"vps{par}"],
                   [f"vps{par}"], tag=f"vmm{kc}")
                dve(lambda e, par=par: e.tensor_tensor(v32p[par][:], vps[par][:],
                                                       wvb_bc[:], op=AL.add),
                    [f"vps{par}", "wvb_bc", f"v32_{par}"], [f"v32_{par}"],
                    tag=f"vb{kc}")

                dve(lambda e, par=par: e.tensor_copy(
                    vaug[0][:].rearrange("p (h w) -> p h w", w=65)[:, :, 0:64],
                    v32p[par][:].rearrange("p (h w) -> p h w", w=64)),
                    [f"v32_{par}", "vaug0r", "vaug0c"], ["vaug0"], tag=f"vhi{kc}")
                dve(lambda e, par=par: e.tensor_tensor(
                    vaug[1][:].rearrange("p (h w) -> p h w", w=65)[:, :, 0:64],
                    v32p[par][:].rearrange("p (h w) -> p h w", w=64),
                    vaug[0][:].rearrange("p (h w) -> p h w", w=65)[:, :, 0:64],
                    op=AL.subtract),
                    [f"v32_{par}", "vaug0", "vaug1r", "vaug1c"], ["vaug1"],
                    tag=f"vlo{kc}")
                dma("scalar", lambda e, kc=kc: e.dma_start(
                    out=d_v[0][kc * 128:(kc + 1) * 128, :, :]
                    .rearrange("p h w -> p (h w)"),
                    in_=vaug[0][:]), reads=["vaug0"], writes=["d_v0"], tag=f"spv0_{kc}")
                pr.last_writer["vaug0r"] = pr.ops[-1]["idx"]
                dma("scalar", lambda e, kc=kc: e.dma_start(
                    out=d_v[1][kc * 128:(kc + 1) * 128, :, :]
                    .rearrange("p h w -> p (h w)"),
                    in_=vaug[1][:]), reads=["vaug1"], writes=["d_v1"], tag=f"spv1_{kc}")
                pr.last_writer["vaug1r"] = pr.ops[-1]["idx"]
            pr.fence()
        for (d_w, d_wqt, nchunks, width, j, wqname) in QSPECS:
            quant_stageBC(d_w, d_wqt, nchunks, width, j, wqname)

        with contextlib.ExitStack() as p4:
            kg = [[sb(f"kg{p}_{t}", [128, S], BF16, st=p4) for t in range(2)]
                  for p in range(2)]
            qg = [[sb(f"qg{p}_{t}", [128, TQ], BF16, st=p4) for t in range(2)]
                  for p in range(2)]
            vg = [sb(f"vg{p}", [128, KC * 260], BF16, st=p4) for p in range(2)]
            p32 = [sb(f"p32_{i}", [128, 1024], st=p4) for i in range(4)]
            phi = [sb(f"phi{i}", [128, 1024], BF16, st=p4) for i in range(4)]
            plo = [sb(f"plo{i}", [128, 1024], BF16, st=p4) for i in range(4)]
            stg = sb("stg", [64, 512], st=p4)
            sumr = sb("sumr", [1, 512], st=p4)
            rb_sb = sb("rb_sb", [64, 512], st=p4)
            oghi = sb("oghi", [64, 512], BF16, st=p4)
            oglo = sb("oglo", [64, 512], BF16, st=p4)
            s_ps = [psum(f"s_ps{i}", [128, 1024], st=p4) for i in range(2)]
            o_psA = psum("o_psA", [65, 512], st=p4)
            o_psB = psum("o_psB", [65, 512], st=p4)
            rb_ps = psum("rb_ps", [64, 512], st=p4)

            for g in range(4):
                for t2 in range(2):
                    for pl in range(2):
                        dma("sync", lambda e, g=g, t2=t2, pl=pl: e.dma_start(
                            out=kg[pl][t2][:],
                            in_=d_h[pl][(g * 2 + t2) * 128:(g * 2 + t2 + 1) * 128, :]),
                            reads=[f"kg{pl}{t2}", "d_h0", "d_h1"],
                            writes=[f"kg{pl}{t2}"], tag=f"ldkg{g}_{t2}_{pl}")
                        dma("sync", lambda e, g=g, t2=t2, pl=pl: e.dma_start(
                            out=qg[pl][t2][:],
                            in_=d_q[pl][(g * 2 + t2) * 128:(g * 2 + t2 + 1) * 128, :]),
                            reads=[f"qg{pl}{t2}", "d_q0", "d_q1"],
                            writes=[f"qg{pl}{t2}"], tag=f"ldqg{g}_{t2}_{pl}")
                for pl in range(2):
                    for kc in range(KC):
                        dma("sync", lambda e, g=g, pl=pl, kc=kc: e.dma_start(
                            out=vg[pl][:, kc * 260:(kc + 1) * 260],
                            in_=d_v[pl][kc * 128:(kc + 1) * 128, g * 4:(g + 1) * 4, :]
                            .rearrange("p h w -> p (h w)")),
                            reads=[f"vg{pl}", "d_v0", "d_v1"], writes=[f"vg{pl}"],
                            tag=f"ldvg{g}_{pl}_{kc}")
                for p2 in range(2):
                    pair = g * 2 + p2

                    def smm(e, p2, q0, kc, par):
                        last = None
                        for pi, (lp, rp) in enumerate(((0, 0), (0, 1), (1, 0))):
                            for hh in range(2):
                                base = 64 * hh
                                last = e.matmul(
                                    s_ps[par][:, hh * 512:(hh + 1) * 512],
                                    kg[lp][p2][base:base + 64, kc * 128:(kc + 1) * 128],
                                    qg[rp][p2][base:base + 64, q0:q0 + 512],
                                    start=(pi == 0), stop=(pi == 2),
                                    tile_position=(base, 0))
                        return last

                    def avmm(e, p2, kc, par):
                        last = None
                        for pi, (lp, rp) in enumerate(((0, 0), (0, 1), (1, 0))):
                            rt = (phi if rp == 0 else plo)[par]
                            for hh in range(2):
                                hcol = kc * 260 + (2 * p2 + hh) * 65
                                ops_ = o_psA if hh == 0 else o_psB
                                last = e.matmul(
                                    ops_[:], vg[lp][:, hcol:hcol + 65],
                                    rt[:, hh * 512:(hh + 1) * 512],
                                    start=(pi == 0 and kc == 0),
                                    stop=(pi == 2 and kc == KC - 1))
                        return last

                    def emit_scores(thq, kc, i):
                        par = i % 2
                        pp4 = i % 4
                        q0 = thq * 512
                        pe(lambda e, q0=q0, kc=kc, par=par, f=smm, p2=p2:
                           f(e, p2, q0, kc, par),
                           [f"kg0{p2}", f"kg1{p2}", f"qg0{p2}", f"qg1{p2}", f"sS{par}"],
                           [f"sS{par}"], tag=f"smm{pair}_{thq}_{kc}")
                        sc(lambda e, par=par, pp4=pp4: e.activation(
                            p32[pp4][:], s_ps[par][:], AF.Exp,
                            bias=negCp[:, pair:pair + 1], scale=0.125),
                           [f"sS{par}", "negCp", f"p32_{pp4}"], [f"p32_{pp4}"],
                           tag=f"exp{pair}_{thq}_{kc}")
                        if i % 2 == 0:
                            sc(lambda e, pp4=pp4: e.activation(
                                phi[pp4][:], p32[pp4][:], AF.Copy),
                               [f"p32_{pp4}", f"phi{pp4}"], [f"phi{pp4}"],
                               tag=f"phic{pair}_{thq}_{kc}")
                        else:
                            dve(lambda e, pp4=pp4: e.tensor_copy(
                                phi[pp4][:], p32[pp4][:]),
                                [f"p32_{pp4}", f"phi{pp4}"], [f"phi{pp4}"],
                                tag=f"phiv{pair}_{thq}_{kc}")
                        dve(lambda e, pp4=pp4: e.tensor_tensor(
                            plo[pp4][:], p32[pp4][:], phi[pp4][:], op=AL.subtract),
                            [f"p32_{pp4}", f"phi{pp4}", f"plo{pp4}"], [f"plo{pp4}"],
                            tag=f"plo{pair}_{thq}_{kc}")

                    def emit_av(thq, kc, i):
                        pp4 = i % 4
                        pe(lambda e, kc=kc, pp4=pp4, f=avmm, p2=p2: f(e, p2, kc, pp4),
                           ["vg0", "vg1", f"phi{pp4}", f"plo{pp4}", "oA", "oB"],
                           ["oA", "oB", f"phi{pp4}", f"plo{pp4}"],
                           tag=f"avmm{pair}_{thq}_{kc}")

                    def emit_evac(thq):
                        q0 = thq * 512
                        for hh in range(2):
                            head = 2 * pair + hh
                            ops_ = o_psA if hh == 0 else o_psB
                            dve(lambda e, ops_=ops_: e.tensor_single_scalar(
                                sumr[:], ops_[64:65, :], 1e-30, op=AL.max),
                                ["oA", "oB", "sumr"], ["sumr"],
                                tag=f"sumc{pair}_{thq}_{hh}")
                            dve(lambda e: e.reciprocal(sumr[:], sumr[:]),
                                ["sumr"], ["sumr"])
                            pe(lambda e: e.matmul(rb_ps[:], ones[0:1, 0:64], sumr[:],
                                                  start=True, stop=True),
                               ["sumr", "ones", "rb_ps"], ["rb_ps"],
                               tag=f"rbmm{pair}_{thq}_{hh}")
                            dve(lambda e: e.tensor_copy(rb_sb[:], rb_ps[:]),
                                ["rb_ps", "rb_sb"], ["rb_sb"],
                                tag=f"rbev{pair}_{thq}_{hh}")
                            dve(lambda e, ops_=ops_: e.tensor_tensor(
                                stg[:], ops_[0:64, :], rb_sb[:], op=AL.mult),
                                ["oA", "oB", "rb_sb", "stg"], ["stg"],
                                tag=f"onrm{pair}_{thq}_{hh}")
                            dve(lambda e: e.tensor_copy(oghi[:], stg[:]),
                                ["stg", "oghi"], ["oghi"])
                            dve(lambda e: e.tensor_tensor(oglo[:], stg[:], oghi[:],
                                                          op=AL.subtract),
                                ["stg", "oghi", "oglo"], ["oglo"])
                            dma("scalar", lambda e, head=head, q0=q0: e.dma_start(
                                out=d_o[0][head * 64:(head + 1) * 64, q0:q0 + 512],
                                in_=oghi[:]), reads=["oghi"], writes=["d_o0"],
                                tag=f"spo0_{pair}_{thq}_{hh}")
                            pr.last_writer["oghi"] = pr.ops[-1]["idx"]
                            dma("scalar", lambda e, head=head, q0=q0: e.dma_start(
                                out=d_o[1][head * 64:(head + 1) * 64, q0:q0 + 512],
                                in_=oglo[:]), reads=["oglo"], writes=["d_o1"],
                                tag=f"spo1_{pair}_{thq}_{hh}")
                            pr.last_writer["oglo"] = pr.ops[-1]["idx"]

                    flat = [(thq, kc) for thq in range(4) for kc in range(KC)]
                    LEAD = 2
                    emitted_av = 0
                    for i in range(len(flat) + LEAD):
                        if i < len(flat):
                            emit_scores(*flat[i], i)
                        if i >= LEAD:
                            j = i - LEAD
                            emit_av(*flat[j], j)
                            if flat[j][1] == KC - 1:
                                emit_evac(flat[j][0])
            pr.fence()
        qs.close()

        p58 = stk.enter_context(contextlib.ExitStack())
        rescale2 = sb("rescale2", [128, 16], st=p58)
        kq2T = sb("kq2T", [128, 16 * E], BF16, st=p58)
        with contextlib.ExitStack() as p5:
            owp = [sb(f"owp{p}", [128, NCH * E], BF16, st=p5) for p in range(2)]
            ow32 = sb("ow32", [128, E], st=p5)
            osl = [[sb(f"osl{p}_{i}", [128, NCH * 512], BF16, st=p5) for i in range(2)]
                   for p in range(2)]
            xo_t = [sb(f"xo_t{i}", [128, E], st=p5) for i in range(2)]
            xp_t = [sb(f"xp_t{i}", [128, E], st=p5) for i in range(2)]
            h2 = [sb(f"h2_{i}", [128, E], st=p5) for i in range(2)]
            sqs = sb("sqs", [128, E], st=p5)
            kq2 = [sb(f"kq2_{i}", [128, E], BF16, st=p5) for i in range(2)]
            sA = [sb(f"sA{i}", [128, 1], st=p5) for i in range(2)]
            sB = [sb(f"sB{i}", [128, 1], st=p5) for i in range(2)]
            sM = [sb(f"sM{i}", [128, 1], st=p5) for i in range(2)]
            sR = [sb(f"sR{i}", [128, 1], st=p5) for i in range(2)]
            o_ps5 = [psum(f"o_ps5{i}", [128, 1024], st=p5) for i in range(2)]
            t_ps5 = psum("t_ps5", [128, 1024], BF16, st=p5)

            for c in range(NCH):
                dma("sync", lambda e, c=c: e.dma_start(
                    out=ow32[:], in_=d_owT[c * 128:(c + 1) * 128, :]),
                    reads=["ow32r"], writes=["ow32"], tag=f"ldow{c}")
                dve(lambda e, c=c: e.tensor_copy(owp[0][:, c * E:(c + 1) * E], ow32[:]),
                    ["ow32"], ["owph"], tag=f"owh{c}")
                dve(lambda e, c=c: e.tensor_tensor(owp[1][:, c * E:(c + 1) * E], ow32[:],
                                                   owp[0][:, c * E:(c + 1) * E],
                                                   op=AL.subtract),
                    ["ow32", "owph"], ["owpl"], tag=f"owl{c}")
                pr.last_writer["ow32r"] = pr.ops[-1]["idx"]
            for t in range(16):
                par = t % 2
                tg = t // 4
                if t % 4 == 0:
                    gpar = tg % 2
                    for pl in range(2):
                        for c in range(NCH):
                            dma("sync", lambda e, tg=tg, pl=pl, c=c, gpar=gpar: e.dma_start(
                                out=osl[pl][gpar][:, c * 512:(c + 1) * 512],
                                in_=d_o[pl][c * 128:(c + 1) * 128,
                                            tg * 512:(tg + 1) * 512]),
                                reads=[f"osl{pl}{gpar}", "d_o0", "d_o1"],
                                writes=[f"osl{pl}{gpar}"], tag=f"ldosl{tg}_{pl}_{c}")
                gpar = tg % 2
                toff = (t % 4) * 128
                dma("sync", lambda e, t=t, par=par: e.dma_start(
                    out=xo_t[par][:], in_=d_xo[t * 128:(t + 1) * 128, :]),
                    reads=[f"xo_t{par}"], writes=[f"xo_t{par}"], tag=f"ldxo{t}")

                def omm(e, gpar=gpar, toff=toff, par=par):
                    last = None
                    for pi, (lp, rp) in enumerate(((0, 0), (0, 1), (1, 0))):
                        for c in range(NCH):
                            for oh in range(2):
                                last = e.matmul(
                                    o_ps5[par][:, oh * 512:(oh + 1) * 512],
                                    osl[lp][gpar][:, c * 512 + toff:c * 512 + toff + 128],
                                    owp[rp][:, c * E + oh * 512:c * E + (oh + 1) * 512],
                                    start=(pi == 0 and c == 0),
                                    stop=(pi == 2 and c == NCH - 1))
                    return last
                pe(omm, [f"osl0{gpar}", f"osl1{gpar}", "owph", "owpl", f"o_ps5{par}"],
                   [f"o_ps5{par}"], tag=f"omm{t}")
                dve(lambda e, par=par: e.tensor_tensor(xp_t[par][:], o_ps5[par][:],
                                                       owb_bc[:], op=AL.add),
                    [f"o_ps5{par}", "owb_bc", f"xp_t{par}"], [f"xp_t{par}"], tag=f"xp1_{t}")
                dve(lambda e, par=par: e.tensor_tensor(xp_t[par][:], xp_t[par][:],
                                                       ga_bc[:], op=AL.mult),
                    [f"xp_t{par}", "ga_bc"], [f"xp_t{par}"])
                dve(lambda e, par=par: e.tensor_tensor(xp_t[par][:], xp_t[par][:],
                                                       xo_t[par][:], op=AL.add),
                    [f"xp_t{par}", f"xo_t{par}"], [f"xp_t{par}"])
                dma("scalar", lambda e, t=t, par=par: e.dma_start(
                    out=d_xp[t * 128:(t + 1) * 128, :], in_=xp_t[par][:]),
                    reads=[f"xp_t{par}"], writes=["d_xp"], tag=f"spxp{t}")
                dve(lambda e, par=par: e.reduce_sum(sA[par][:], xp_t[par][:], axis=AX.X),
                    [f"xp_t{par}", f"sA{par}"], [f"sA{par}"], tag=f"ln2s{t}")
                sc(lambda e, par=par: e.activation(sqs[:], xp_t[par][:], AF.Square,
                                                   accum_out=sB[par][:]),
                   [f"xp_t{par}", f"sB{par}", "sqs"], [f"sB{par}", "sqs"], tag=f"ln2q{t}")
                dve(lambda e, par=par: e.tensor_single_scalar(sM[par][:], sA[par][:],
                                                              1.0 / E, op=AL.mult),
                    [f"sA{par}", f"sM{par}"], [f"sM{par}"])
                dve(lambda e, par=par: e.tensor_tensor(sR[par][:], sM[par][:], sM[par][:],
                                                       op=AL.mult),
                    [f"sM{par}", f"sR{par}"], [f"sR{par}"])
                dve(lambda e, par=par: e.scalar_tensor_tensor(sR[par][:], sB[par][:],
                                                              1.0 / E, sR[par][:],
                                                              op0=AL.mult,
                                                              op1=AL.subtract),
                    [f"sB{par}", f"sR{par}"], [f"sR{par}"])
                sc(lambda e, par=par: e.activation(sR[par][:], sR[par][:], AF.Sqrt,
                                                   bias=epscol[:]),
                   [f"sR{par}", "epscol"], [f"sR{par}"], tag=f"ln2std{t}")
                dve(lambda e, par=par: e.reciprocal(sR[par][:], sR[par][:]),
                    [f"sR{par}"], [f"sR{par}"])
                dve(lambda e, par=par: e.tensor_scalar(h2[par][:], xp_t[par][:], sM[par][:],
                                                       sR[par][:], op0=AL.subtract,
                                                       op1=AL.mult),
                    [f"xp_t{par}", f"sM{par}", f"sR{par}", f"h2_{par}"], [f"h2_{par}"],
                    tag=f"ln2n{t}")
                dve(lambda e, par=par: e.tensor_tensor(h2[par][:], h2[par][:], cb1_bc[:],
                                                       op=AL.mult),
                    [f"h2_{par}", "cb1_bc"], [f"h2_{par}"])
                dve(lambda e, par=par: e.tensor_tensor(h2[par][:], h2[par][:], sbb_bc[:],
                                                       op=AL.add),
                    [f"h2_{par}", "sbb_bc"], [f"h2_{par}"])
                dve(lambda e, par=par: e.reduce_max(sA[par][:], h2[par][:], axis=AX.X,
                                                    apply_absolute_value=True),
                    [f"h2_{par}", f"sA{par}"], [f"sA{par}"], tag=f"am2_{t}")
                dve(lambda e, par=par: e.tensor_single_scalar(sA[par][:], sA[par][:], 1e-5,
                                                              op=AL.max),
                    [f"sA{par}"], [f"sA{par}"])
                dve(lambda e, t=t, par=par: e.tensor_single_scalar(
                    rescale2[:, t:t + 1], sA[par][:], 1.0 / 127.0, op=AL.mult),
                    [f"sA{par}"], ["rescale2"], tag=f"rsc2_{t}")
                dve(lambda e, par=par: e.reciprocal(sB[par][:], sA[par][:]),
                    [f"sA{par}", f"sB{par}"], [f"sB{par}"])
                dve(lambda e, par=par: e.tensor_single_scalar(sB[par][:], sB[par][:], 127.0,
                                                              op=AL.mult),
                    [f"sB{par}"], [f"sB{par}"])
                dve(lambda e, par=par: e.tensor_scalar(h2[par][:], h2[par][:], sB[par][:],
                                                       0.0, op0=AL.mult, op1=AL.add),
                    [f"h2_{par}", f"sB{par}"], [f"h2_{par}"])
                dve(lambda e, par=par: e.tensor_scalar(kq2[par][:], h2[par][:], magicC[:],
                                                       magicC[:], op0=AL.add,
                                                       op1=AL.subtract),
                    [f"h2_{par}", "magicC", f"kq2_{par}"], [f"kq2_{par}"],
                    tag=f"round2_{t}")

                def tmm(e, par=par):
                    last = None
                    for c in range(NCH):
                        last = e.transpose(t_ps5[:, c * 128:(c + 1) * 128],
                                           kq2[par][:, c * 128:(c + 1) * 128], identb[:])
                    return last
                pe(tmm, [f"kq2_{par}", "identb", "t_ps5"], ["t_ps5"], tag=f"kq2T{t}")
                dve(lambda e, t=t: e.tensor_copy(kq2T[:, t * E:(t + 1) * E], t_ps5[:]),
                    ["t_ps5"], ["kq2T"], tag=f"kq2Tev{t}")
            pr.fence()

        TB = 2
        with contextlib.ExitStack() as p8:
            wgs = [sb(f"wgs{i}", [128, NCH * 512], BF16, st=p8) for i in range(2)]
            wus = [sb(f"wus{i}", [128, NCH * 512], BF16, st=p8) for i in range(2)]
            wds = [sb(f"wds{i}", [128, 8 * 512], BF16, st=p8) for i in range(2)]
            gu = [sb(f"gu{i}", [128, INNER], st=p8) for i in range(TB)]
            sil = [sb(f"sil{i}", [128, 512], st=p8) for i in range(2)]
            upv = [sb(f"upv{i}", [128, 512], st=p8) for i in range(2)]
            kq3 = sb("kq3", [128, INNER], BF16, st=p8)
            kq3T = [sb(f"kq3T{i}", [128, INNER], BF16, st=p8) for i in range(TB)]
            xp8 = sb("xp8", [128, E], st=p8)
            yt = [sb(f"yt{i}", [128, E], st=p8) for i in range(TB)]
            a1 = sb("a1", [128, 1], st=p8)
            a2 = sb("a2", [128, 1], st=p8)
            gsc = sb("gsc", [128, 16], st=p8)
            usc = sb("usc", [128, 16], st=p8)
            dsc = sb("dsc", [128, 16], st=p8)
            g_ps = [psum("g_ps0", [128, 512], st=p8)]
            u_ps = [psum("u_ps0", [128, 512], st=p8)]
            dn_ps = psum("dn_ps", [128, 2048], st=p8)
            t_ps8_t = psum("t_ps8", [128, 2048], BF16, st=p8)
            t_ps8 = t_ps8_t[:]

            dve(lambda e: e.tensor_scalar(gsc[:], rescale2[:], swq[:, 0:1], 0.0,
                                          op0=AL.mult, op1=AL.add),
                ["rescale2", "swq"], ["gsc"])
            dve(lambda e: e.tensor_scalar(usc[:], rescale2[:], swq[:, 1:2], 0.0,
                                          op0=AL.mult, op1=AL.add),
                ["rescale2", "swq"], ["usc"])
            for tb in range(16 // TB):
                for ot in range(8):
                    wpar = ot % 2
                    for c in range(NCH):
                        dma("sync", lambda e, ot=ot, wpar=wpar, c=c: e.dma_start(
                            out=wgs[wpar][:, c * 512:(c + 1) * 512],
                            in_=d_wq[0][c, :, ot * 512:(ot + 1) * 512]),
                            reads=[f"wgsr{wpar}", "d_wgq"], writes=[f"wgs{wpar}"],
                            tag=f"ldwg{tb}_{ot}_{c}")
                        dma("gpsimd", lambda e, ot=ot, wpar=wpar, c=c: e.dma_start(
                            out=wus[wpar][:, c * 512:(c + 1) * 512],
                            in_=d_wq[1][c, :, ot * 512:(ot + 1) * 512]),
                            reads=[f"wusr{wpar}", "d_wuq"], writes=[f"wus{wpar}"],
                            tag=f"ldwu{tb}_{ot}_{c}")
                    for ti in range(TB):
                        t = tb * TB + ti
                        gp_ = 0

                        def gumm(e, t=t, wpar=wpar, gp_=gp_):
                            last = None
                            for c in range(NCH):
                                lhs = kq2T[:, t * E + c * 128:t * E + (c + 1) * 128]
                                e.matmul(g_ps[gp_][:], lhs,
                                         wgs[wpar][:, c * 512:(c + 1) * 512],
                                         start=(c == 0), stop=(c == NCH - 1))
                                last = e.matmul(u_ps[gp_][:], lhs,
                                                wus[wpar][:, c * 512:(c + 1) * 512],
                                                start=(c == 0), stop=(c == NCH - 1))
                            return last
                        pe(gumm, ["kq2T", f"wgs{wpar}", f"wus{wpar}", f"g_ps{gp_}",
                                  f"u_ps{gp_}"],
                           [f"g_ps{gp_}", f"u_ps{gp_}"], tag=f"gumm{tb}_{ot}_{ti}")
                        sc(lambda e, t=t, gp_=gp_: e.activation(
                            sil[gp_][:], g_ps[gp_][:], AF.Silu, scale=gsc[:, t:t + 1]),
                           [f"g_ps{gp_}", "gsc", f"sil{gp_}"], [f"sil{gp_}"],
                           tag=f"silu{tb}_{ot}_{ti}")
                        dve(lambda e, t=t, gp_=gp_: e.tensor_scalar(
                            upv[gp_][:], u_ps[gp_][:], usc[:, t:t + 1], 0.0,
                            op0=AL.mult, op1=AL.add),
                            [f"u_ps{gp_}", "usc", f"upv{gp_}"], [f"upv{gp_}"],
                            tag=f"us{tb}_{ot}_{ti}")
                        pr.last_writer[f"wgsr{wpar}"] = pr.ops[-1]["idx"]
                        pr.last_writer[f"wusr{wpar}"] = pr.ops[-1]["idx"]
                        dve(lambda e, ti=ti, ot=ot, gp_=gp_: e.tensor_tensor(
                            gu[ti][:, ot * 512:(ot + 1) * 512], sil[gp_][:], upv[gp_][:],
                            op=AL.mult), [f"sil{gp_}", f"upv{gp_}", f"gur{ti}"],
                            [f"gu{ti}_{ot}", f"sil{gp_}", f"upv{gp_}"],
                            tag=f"gu{tb}_{ot}_{ti}")
                for ti in range(TB):
                    t = tb * TB + ti
                    gu_all = [f"gu{ti}_{ot}" for ot in range(8)]
                    dve(lambda e, ti=ti: e.reduce_max(a1[:], gu[ti][:], axis=AX.X,
                                                      apply_absolute_value=True),
                        gu_all + ["a1r"], ["a1"], tag=f"am3_{tb}_{ti}")
                    dve(lambda e: e.tensor_single_scalar(a1[:], a1[:], 1e-5, op=AL.max),
                        ["a1"], ["a1"])
                    dve(lambda e, t=t: e.tensor_scalar(dsc[:, t:t + 1], a1[:],
                                                       swq[:, 2:3], 1.0 / 127.0,
                                                       op0=AL.mult, op1=AL.mult),
                        ["a1", "swq"], ["dsc"], tag=f"dsc{tb}_{ti}")
                    dve(lambda e: e.reciprocal(a2[:], a1[:]), ["a1", "a2r"], ["a2"])
                    pr.last_writer["a1r"] = pr.ops[-1]["idx"]
                    dve(lambda e: e.tensor_single_scalar(a2[:], a2[:], 127.0, op=AL.mult),
                        ["a2"], ["a2"])
                    dve(lambda e, ti=ti: e.tensor_scalar(gu[ti][:], gu[ti][:], a2[:], 0.0,
                                                         op0=AL.mult, op1=AL.add),
                        ["a2"] + gu_all, [f"guv{ti}"], tag=f"guv{tb}_{ti}")
                    pr.last_writer["a2r"] = pr.ops[-1]["idx"]
                    dve(lambda e, ti=ti: e.tensor_scalar(kq3[:], gu[ti][:], magicC[:],
                                                         magicC[:], op0=AL.add,
                                                         op1=AL.subtract),
                        [f"guv{ti}", "magicC", "kq3r"], ["kq3", f"gur{ti}"],
                        tag=f"round3_{tb}_{ti}")
                    for half in range(2):
                        def t3mm(e, half=half):
                            last = None
                            for c in range(16):
                                last = e.transpose(
                                    t_ps8[:, c * 128:(c + 1) * 128],
                                    kq3[:, (half * 16 + c) * 128:
                                         (half * 16 + c + 1) * 128], identb[:])
                            return last
                        pe(t3mm, ["kq3", "identb", "t_ps8"], ["t_ps8"],
                           tag=f"kq3T{tb}_{ti}_{half}")
                        dve(lambda e, ti=ti, half=half: e.tensor_copy(
                            kq3T[ti][:, half * 2048:(half + 1) * 2048],
                            t_ps8[:, 0:2048]),
                            ["t_ps8", f"kq3T{ti}"], [f"kq3T{ti}", "t_ps8"],
                            tag=f"kq3Tev{tb}_{ti}_{half}")
                    pr.last_writer["kq3r"] = pr.ops[-1]["idx"]
                for oh in range(2):
                    for cq in range(4):
                        wp = cq % 2
                        for c in range(8):
                            cc2 = cq * 8 + c
                            dma("sync" if wp == 0 else "gpsimd",
                                lambda e, oh=oh, c=c, cc2=cc2, wp=wp: e.dma_start(
                                    out=wds[wp][:, c * 512:(c + 1) * 512],
                                    in_=d_wq[2][cc2, :, oh * 512:(oh + 1) * 512]),
                                reads=[f"wds{wp}", "d_wdq"], writes=[f"wds{wp}"],
                                tag=f"ldwd{tb}_{oh}_{cc2}")
                        for ti in range(TB):
                            def dmm(e, ti=ti, oh=oh, cq=cq, wp=wp):
                                last = None
                                for c in range(8):
                                    last = e.matmul(
                                        dn_ps[:, (ti * 2 + oh) * 512:
                                              (ti * 2 + oh + 1) * 512],
                                        kq3T[ti][:, (cq * 8 + c) * 128:
                                                  (cq * 8 + c + 1) * 128],
                                        wds[wp][:, c * 512:(c + 1) * 512],
                                        start=(cq == 0 and c == 0),
                                        stop=(cq == 3 and c == 7))
                                return last
                            pe(dmm, [f"kq3T{ti}", f"wds{wp}", "dn_ps"],
                               ["dn_ps", f"wds{wp}"],
                               tag=f"dmm{tb}_{ti}_{oh}_{cq}")
                    for ti in range(TB):
                        t = tb * TB + ti
                        dve(lambda e, ti=ti, oh=oh, t=t: e.tensor_scalar(
                            yt[ti][:, oh * 512:(oh + 1) * 512],
                            dn_ps[:, (ti * 2 + oh) * 512:(ti * 2 + oh + 1) * 512],
                            dsc[:, t:t + 1], 0.0, op0=AL.mult, op1=AL.add),
                            ["dn_ps", "dsc", f"yt{ti}_{oh}"], [f"yt{ti}_{oh}"],
                            tag=f"y1_{tb}_{ti}_{oh}")
                for ti in range(TB):
                    t = tb * TB + ti
                    dma("sync", lambda e, t=t: e.dma_start(
                        out=xp8[:], in_=d_xp[t * 128:(t + 1) * 128, :]),
                        reads=["xp8r", "d_xp"], writes=["xp8"], tag=f"ldxp8{t}")
                    dve(lambda e, ti=ti: e.tensor_tensor(yt[ti][:], yt[ti][:], gb_bc[:],
                                                         op=AL.mult),
                        [f"yt{ti}_0", f"yt{ti}_1", "gb_bc"], [f"yt{ti}"],
                        tag=f"y2_{tb}_{ti}")
                    dve(lambda e, ti=ti: e.tensor_tensor(yt[ti][:], yt[ti][:], xp8[:],
                                                         op=AL.add),
                        [f"yt{ti}", "xp8"], [f"yt{ti}"], tag=f"y3_{tb}_{ti}")
                    pr.last_writer["xp8r"] = pr.ops[-1]["idx"]
                    dma("scalar", lambda e, t=t, ti=ti: e.dma_start(
                        out=d_y[t * 128:(t + 1) * 128, :], in_=yt[ti][:]),
                        reads=[f"yt{ti}"], writes=["d_y"], tag=f"sty{t}")
                    pr.last_writer[f"yt{ti}_0"] = pr.ops[-1]["idx"]
                    pr.last_writer[f"yt{ti}_1"] = pr.ops[-1]["idx"]

        counters = pr.finalize()
        lane_names = list(ENGINES) + [f"{q}_d{i}" for q in ENGINES
                                      for i in range(Prog.NLANES)
                                      if any(l == f"{q}_d{i}" for l in counters)]
        with contextlib.ExitStack() as semstack:
            sems = {n: semstack.enter_context(nc.semaphore("sem_" + n))
                    for n in sorted(set(lane_names) | set(counters))}
            with nc.Block() as block:
                @block.sync
                def _(eng):
                    pr.emit_engine("sync", eng, sems)

                @block.vector
                def _(eng):
                    pr.emit_engine("vector", eng, sems)

                @block.scalar
                def _(eng):
                    pr.emit_engine("scalar", eng, sems)

                @block.tensor
                def _(eng):
                    pr.emit_engine("tensor", eng, sems)

                @block.gpsimd
                def _(eng):
                    pr.emit_engine("gpsimd", eng, sems)
                    for ln, tot in sorted(counters.items()):
                        if "_d" in ln and tot:
                            eng.wait_ge(sems[ln], tot)
    return nc


_NC_CACHE = {}


def _get_program():
    if "nc" not in _NC_CACHE:
        _NC_CACHE["nc"] = build_program()
    return _NC_CACHE["nc"]


def kernel(L0_attn_x, L0_attn_w_att_val, L0_attn_wv_w, L0_attn_wv_b, L0_attn_out_w,
           L0_attn_out_b, L0_ffn_w_gate, L0_ffn_w_up, L0_ffn_w_down, L0_c, L0_wmod):
    x = np.ascontiguousarray(np.asarray(L0_attn_x, np.float32))
    wav = np.asarray(L0_attn_w_att_val, np.float32)
    wavbd = np.zeros((NCH, 128, 128), np.float32)
    for c in range(NCH):
        wavbd[c, 0:64, 0:64] = wav[2 * c]
        wavbd[c, 64:128, 64:128] = wav[2 * c + 1]
    common = {
        "wav": wavbd,
        "wvT": np.ascontiguousarray(np.asarray(L0_attn_wv_w, np.float32).T),
        "wvb": np.asarray(L0_attn_wv_b, np.float32).reshape(1, E),
        "owT": np.ascontiguousarray(np.asarray(L0_attn_out_w, np.float32).T),
        "owb": np.asarray(L0_attn_out_b, np.float32).reshape(1, E),
        "wgT": np.ascontiguousarray(np.asarray(L0_ffn_w_gate, np.float32).T),
        "wuT": np.ascontiguousarray(np.asarray(L0_ffn_w_up, np.float32).T),
        "wdT": np.ascontiguousarray(np.asarray(L0_ffn_w_down, np.float32).T),
        "wmod": np.ascontiguousarray(np.asarray(L0_wmod, np.float32)),
        "ones128": np.ones((128, 128), np.float32),
        "onesA": np.concatenate([np.ones((64, 128), np.float32),
                                 np.zeros((64, 128), np.float32)], axis=0),
        "onesB": np.concatenate([np.zeros((64, 128), np.float32),
                                 np.ones((64, 128), np.float32)], axis=0),
        "ident128": np.eye(128, dtype=np.float32),
    }
    cmat = np.asarray(L0_c, np.float32)
    in_maps = []
    for core in range(8):
        b, half = core // 2, core % 2
        xb = x[b]
        roll = np.roll(np.arange(S), -half * TQ)
        im = dict(common)
        im["xT"] = np.ascontiguousarray(xb[roll].T)
        im["xo"] = np.ascontiguousarray(xb[half * TQ:(half + 1) * TQ])
        im["cc"] = np.ascontiguousarray(cmat[b].reshape(NCH, 128).T)
        in_maps.append(im)

    nc = _get_program()
    res = run_bass_kernel_spmd(nc, in_maps, core_ids=list(range(8)))
    out = np.empty((B, S, E), np.float32)
    for core in range(8):
        b, half = core // 2, core % 2
        out[b, half * TQ:(half + 1) * TQ] = res.results[core]["y"]
    return out


# version 32 (final)
# speedup vs baseline: 1.2209x; 1.2209x over previous
import contextlib
from collections import defaultdict

import numpy as np
import concourse.bass as bass
import concourse.mybir as mybir
from concourse.alu_op_type import AluOpType
from concourse.bass_utils import run_bass_kernel_spmd

F32 = mybir.dt.float32
BF16 = mybir.dt.bfloat16
AF = mybir.ActivationFunctionType
AX = mybir.AxisListType
AL = AluOpType

B, S, E, H = 4, 4096, 1024, 16
TQ = 2048
NCH = E // 128
KC = S // 128
INNER = 4096

MAGIC32 = 12582912.0
MAGICD = float(1.5 * 2**52 - 1.5 * 2**23)

DMA_INC = 16
ENGINES = ("tensor", "vector", "scalar", "gpsimd", "sync")


class Prog:

    NLANES = 8

    def __init__(self):
        self.ops = []
        self.last_writer = {}
        self.readers = defaultdict(list)
        self.fence_deps = []
        self.dma_count = defaultdict(int)

    def emit(self, engine, fn, reads=(), writes=(), dma=False, tag=""):
        idx = len(self.ops)
        deps = set(self.fence_deps)
        for r in reads:
            w = self.last_writer.get(r)
            if w is not None:
                deps.add(w)
        for r in writes:
            w = self.last_writer.get(r)
            if w is not None:
                deps.add(w)
            for rd in self.readers[r]:
                deps.add(rd)
        deps.discard(idx)
        if engine == "tensor" and not dma:
            deps = {d for d in deps
                    if not (self.ops[d]["engine"] == "tensor" and not self.ops[d]["dma"])}
        if dma:
            lane = f"{engine}_d{self.dma_count[engine] % self.NLANES}"
            self.dma_count[engine] += 1
        else:
            lane = engine
        self.ops.append(dict(idx=idx, engine=engine, lane=lane, fn=fn, dma=dma,
                             deps=deps, tag=tag, inc_needed=bool(dma), seq=None))
        for r in writes:
            self.last_writer[r] = idx
            self.readers[r] = []
        for r in reads:
            self.readers[r].append(idx)
        return idx

    def fence(self):
        last = {}
        for op in self.ops:
            last[op["lane"]] = op["idx"]
        self.fence_deps = list(last.values())

    def finalize(self):
        for op in self.ops:
            for d in op["deps"]:
                self.ops[d]["inc_needed"] = True
        counters = defaultdict(int)
        for op in self.ops:
            if op["inc_needed"]:
                counters[op["lane"]] += DMA_INC if op["dma"] else 1
                op["seq"] = counters[op["lane"]]
        self.lane_totals = dict(counters)
        return counters

    def emit_engine(self, name, eng, sems):
        watermark = defaultdict(int)
        lane_prev = defaultdict(int)
        for op in self.ops:
            if op["engine"] != name:
                continue
            need = {}
            for d in op["deps"]:
                dop = self.ops[d]
                if dop["seq"] is None:
                    continue
                ln = dop["lane"]
                if dop["seq"] > need.get(ln, 0):
                    need[ln] = dop["seq"]
            if op["dma"] and op["inc_needed"]:
                prev = lane_prev[op["lane"]]
                if prev > need.get(op["lane"], 0):
                    need[op["lane"]] = prev
            for ln, v in sorted(need.items()):
                if watermark[ln] < v:
                    eng.wait_ge(sems[ln], v)
                    watermark[ln] = v
            inst = op["fn"](eng)
            if op["inc_needed"]:
                if inst is None:
                    raise RuntimeError(f"op {op['idx']} ({op['tag']}) needs inc, got None")
                inst.then_inc(sems[op["lane"]], DMA_INC if op["dma"] else 1)
                lane_prev[op["lane"]] += DMA_INC if op["dma"] else 1


def build_program():
    nc = bass.Bass("TRN2", target_bir_lowering=False)

    d_xT = nc.dram_tensor("xT", [E, S], F32, kind="ExternalInput").ap()
    d_xo = nc.dram_tensor("xo", [TQ, E], F32, kind="ExternalInput").ap()
    d_cc = nc.dram_tensor("cc", [128, NCH], F32, kind="ExternalInput").ap()
    d_wmod = nc.dram_tensor("wmod", [E, 6 * E], F32, kind="ExternalInput").ap()
    d_wav = nc.dram_tensor("wav", [NCH, 128, 128], F32, kind="ExternalInput").ap()
    d_wvT = nc.dram_tensor("wvT", [E, E], F32, kind="ExternalInput").ap()
    d_wvb = nc.dram_tensor("wvb", [1, E], F32, kind="ExternalInput").ap()
    d_owT = nc.dram_tensor("owT", [E, E], F32, kind="ExternalInput").ap()
    d_owb = nc.dram_tensor("owb", [1, E], F32, kind="ExternalInput").ap()
    d_wgT = nc.dram_tensor("wgT", [E, INNER], F32, kind="ExternalInput").ap()
    d_wuT = nc.dram_tensor("wuT", [E, INNER], F32, kind="ExternalInput").ap()
    d_wdT = nc.dram_tensor("wdT", [INNER, E], F32, kind="ExternalInput").ap()
    d_ones = nc.dram_tensor("ones128", [128, 128], F32, kind="ExternalInput").ap()
    d_onesA = nc.dram_tensor("onesA", [128, 128], F32, kind="ExternalInput").ap()
    d_onesB = nc.dram_tensor("onesB", [128, 128], F32, kind="ExternalInput").ap()
    d_ident = nc.dram_tensor("ident128", [128, 128], F32, kind="ExternalInput").ap()
    d_y = nc.dram_tensor("y", [TQ, E], F32, kind="ExternalOutput").ap()

    d_h = [nc.dram_tensor(f"d_h{p}", [E, S], BF16).ap() for p in range(2)]
    d_q = [nc.dram_tensor(f"d_q{p}", [E, TQ], BF16).ap() for p in range(2)]
    d_v = [nc.dram_tensor(f"d_v{p}", [S, H, 65], BF16).ap() for p in range(2)]
    d_o = [nc.dram_tensor(f"d_o{p}", [E, TQ], BF16).ap() for p in range(2)]
    d_xp = nc.dram_tensor("d_xp", [TQ, E], F32).ap()
    d_wq = [nc.dram_tensor(n, [NCH if n != "d_wdq" else 32, 128, INNER if n != "d_wdq" else E],
                           BF16).ap() for n in ("d_wgq", "d_wuq", "d_wdq")]

    pr = Prog()
    stk = contextlib.ExitStack()

    def sb(name, shape, dtype=F32, st=None):
        return (st or stk).enter_context(nc.sbuf_tensor(name, shape, dtype))

    def psum(name, shape, dtype=F32, st=None):
        return (st or stk).enter_context(nc.psum_tensor(name, shape, dtype))

    def dve(fn, reads=(), writes=(), tag=""):
        return pr.emit("vector", fn, reads, writes, tag=tag)

    def gp(fn, reads=(), writes=(), tag=""):
        return pr.emit("gpsimd", fn, reads, writes, tag=tag)

    def sc(fn, reads=(), writes=(), tag=""):
        return pr.emit("scalar", fn, reads, writes, tag=tag)

    def pe(fn, reads=(), writes=(), tag=""):
        return pr.emit("tensor", fn, reads, writes, tag=tag)

    def dma(q, fn, reads=(), writes=(), tag=""):
        return pr.emit(q, fn, reads, writes, dma=True, tag=tag)

    with stk:
        ones = sb("ones", [128, 128])
        onesAb = sb("onesAb", [128, 128], BF16)
        onesBb = sb("onesBb", [128, 128], BF16)
        identb = sb("identb", [128, 128], BF16)
        magicC = sb("magicC", [128, 1])
        epscol = sb("epscol", [128, 1])
        ca1c = sb("ca1c", [128, NCH])
        sac = sb("sac", [128, NCH])
        ga_bc = sb("ga_bc", [128, E])
        gb_bc = sb("gb_bc", [128, E])
        cb1_bc = sb("cb1_bc", [128, E])
        sbb_bc = sb("sbb_bc", [128, E])
        owb_bc = sb("owb_bc", [128, E])
        wvb_bc = sb("wvb_bc", [128, E])
        negC = sb("negC", [128, H])
        negCp = sb("negCp", [128, H // 2])
        mqsq = sb("mqsq", [128, H])
        mksq = sb("mksq", [128, H])
        swq = sb("swq", [128, 3])

        dma("sync", lambda e: e.dma_start(out=ones[:], in_=d_ones[:]), writes=["ones"])
        p0s = contextlib.ExitStack()
        tmpA = sb("tmpA", [128, 128], st=p0s)
        dma("sync", lambda e: e.dma_start(out=tmpA[:], in_=d_onesA[:]), writes=["tA"])
        dve(lambda e: e.tensor_copy(onesAb[:], tmpA[:]), ["tA"], ["onesAb"])
        dma("sync", lambda e: e.dma_start(out=tmpA[:], in_=d_onesB[:]), writes=["tA"])
        dve(lambda e: e.tensor_copy(onesBb[:], tmpA[:]), ["tA"], ["onesBb"])
        dma("sync", lambda e: e.dma_start(out=tmpA[:], in_=d_ident[:]), writes=["tA"])
        dve(lambda e: e.tensor_copy(identb[:], tmpA[:]), ["tA"], ["identb"])
        dve(lambda e: e.tensor_scalar(magicC[:], ones[:, 0:1], 2.0 ** -40, 1.0,
                                      op0=AL.mult, op1=AL.add), ["ones"], ["mz0"])
        dve(lambda e: e.tensor_single_scalar(magicC[:], magicC[:], 1.0, op=AL.is_gt),
            ["mz0"], ["mz1"])
        dve(lambda e: e.tensor_scalar(magicC[:], magicC[:], MAGICD, MAGIC32,
                                      op0=AL.mult, op1=AL.add), ["mz1"], ["magicC"])
        dve(lambda e: e.tensor_scalar(epscol[:], ones[:, 0:1], 1e-5, 0.0,
                                      op0=AL.mult, op1=AL.add), ["ones"], ["epscol"])

        with contextlib.ExitStack() as p1:
            cc = sb("cc_sb", [128, NCH], st=p1)
            scol = sb("scol", [128, NCH], st=p1)
            wm = [sb(f"wm{i}", [128, 3072], st=p1) for i in range(2)]
            modrow = sb("modrow", [1, 6 * E], st=p1)
            mt1 = sb("mt1", [1, E], st=p1)
            mt2 = sb("mt2", [1, E], st=p1)
            dma("sync", lambda e: e.dma_start(
                out=cc[:], in_=d_cc[:]), writes=["cc"])
            sc(lambda e: e.activation(scol[:], cc[:], AF.Silu), ["cc"], ["scol"])
            with contextlib.ExitStack() as p1a:
                mps = psum("mps", [1, 3072], st=p1a)
                for half in range(2):
                    for c in range(NCH):
                        buf = wm[c % 2]
                        dma("sync", lambda e, c=c, half=half, buf=buf: e.dma_start(
                            out=buf[:], in_=d_wmod[c * 128:(c + 1) * 128,
                                                   half * 3072:(half + 1) * 3072]),
                            reads=[f"wmmm{c % 2}"], writes=[f"wm{c % 2}"],
                            tag=f"ldwmod{half}_{c}")

                        def modmm(e, c=c, buf=buf):
                            last = None
                            for nt in range(6):
                                last = e.matmul(mps[:, nt * 512:(nt + 1) * 512],
                                                scol[:, c:c + 1],
                                                buf[:, nt * 512:(nt + 1) * 512],
                                                start=(c == 0), stop=(c == NCH - 1))
                            return last
                        pe(modmm, ["scol", f"wm{c % 2}", "mps"],
                           ["mps", f"wmmm{c % 2}"], tag=f"modmm{half}_{c}")
                    dve(lambda e, half=half: e.tensor_copy(
                        modrow[:, half * 3072:(half + 1) * 3072], mps[:]),
                        ["mps"], ["modrow"], tag=f"modev{half}")
            pr.fence()
            with contextlib.ExitStack() as p1b:
                tps = psum("tps", [128, 128], st=p1b)
                idf = sb("idf", [128, 128], st=p1b)
                dma("sync", lambda e: e.dma_start(out=idf[:], in_=d_ident[:]), writes=["idf"])
                for j, dst, nm in ((0, sac, "sac"), (1, ca1c, "ca1c")):
                    for c in range(NCH):
                        pe(lambda e, j=j, c=c: e.transpose(
                            tps[:, 0:1], modrow[:, j * E + c * 128:j * E + (c + 1) * 128],
                            idf[0:1, 0:1]), ["modrow", "idf", nm], ["tpsr"],
                           tag=f"modT{j}_{c}")
                        dve(lambda e, dst=dst, c=c: e.tensor_copy(dst[:, c:c + 1],
                                                                  tps[:, 0:1]),
                            ["tpsr"], [nm], tag=f"modTev{j}_{c}")
                dve(lambda e: e.tensor_single_scalar(ca1c[:], ca1c[:], 1.0, op=AL.add),
                    ["ca1c"], ["ca1c"])
            pr.fence()
            with contextlib.ExitStack() as p1c:
                bps = psum("bps", [128, 1024], st=p1c)

                def bcast(srcap, dst, dstname, readset):
                    def mm(e, srcap=srcap):
                        last = None
                        for nt in range(2):
                            last = e.matmul(bps[:, nt * 512:(nt + 1) * 512], ones[0:1, :],
                                            srcap[:, nt * 512:(nt + 1) * 512],
                                            start=True, stop=True)
                        return last
                    pe(mm, list(readset) + ["ones", "bpsr"], ["bps"], tag=f"bc_{dstname}")
                    dve(lambda e, dst=dst: e.tensor_copy(dst[:], bps[:]),
                        ["bps"], [dstname, "bpsr"], tag=f"bcev_{dstname}")

                def clampg(off):
                    dve(lambda e, off=off: e.tensor_scalar(
                        mt1[:], modrow[:, off:off + E], 0.0, 2.0, op0=AL.is_ge,
                        op1=AL.mult), ["modrow", "mt1r"], ["mt1"], tag=f"cg1_{off}")
                    dve(lambda e: e.tensor_single_scalar(mt1[:], mt1[:], 1.0,
                                                         op=AL.subtract), ["mt1"], ["mt1"])
                    sc(lambda e, off=off: e.activation(mt2[:], modrow[:, off:off + E],
                                                       AF.Abs),
                       ["modrow", "mt2r"], ["mt2"], tag=f"cg2_{off}")
                    dve(lambda e: e.tensor_single_scalar(mt2[:], mt2[:], 0.25, op=AL.max),
                        ["mt2"], ["mt2"])
                    dve(lambda e: e.tensor_tensor(mt2[:], mt2[:], mt1[:], op=AL.mult),
                        ["mt2", "mt1"], ["mt2", "mt1r"])

                clampg(2 * E)
                bcast(mt2, ga_bc, "ga_bc", ["mt2"])
                pr.last_writer["mt2r"] = pr.ops[-1]["idx"]
                clampg(5 * E)
                bcast(mt2, gb_bc, "gb_bc", ["mt2"])
                pr.last_writer["mt2r"] = pr.ops[-1]["idx"]
                dve(lambda e: e.tensor_single_scalar(mt1[:], modrow[:, 4 * E:5 * E], 1.0,
                                                     op=AL.add),
                    ["modrow", "mt1r"], ["mt1"], tag="cb1")
                bcast(mt1, cb1_bc, "cb1_bc", ["mt1"])
                pr.last_writer["mt1r"] = pr.ops[-1]["idx"]
                bcast(modrow[:, 3 * E:4 * E], sbb_bc, "sbb_bc", ["modrow"])
                vbr = sb("vbr", [1, E], st=p1c)
                obr = sb("obr", [1, E], st=p1c)
                dma("sync", lambda e: e.dma_start(out=vbr[:], in_=d_wvb[:]), writes=["vbr"])
                dma("sync", lambda e: e.dma_start(out=obr[:], in_=d_owb[:]), writes=["obr"])
                bcast(vbr, wvb_bc, "wvb_bc", ["vbr"])
                bcast(obr, owb_bc, "owb_bc", ["obr"])
            pr.fence()
        p0s.close()

        qs = contextlib.ExitStack()
        qwb = [sb(f"qwb{i}", [128, 2048], st=qs) for i in range(2)]
        qpos = sb("qpos", [128, 2048], st=qs)
        qwq = sb("qwq", [128, 2048], BF16, st=qs)
        qacc = [sb(f"qacc{j}", [128, 64], st=qs) for j in range(3)]
        qtot = [sb(f"qtot{j}", [128, 1], st=qs) for j in range(3)]
        qrow = sb("qrow", [1, 512], st=qs)
        QSPECS = [(d_wgT, d_wq[0], NCH, INNER, 0, "d_wgq"),
                  (d_wuT, d_wq[1], NCH, INNER, 1, "d_wuq"),
                  (d_wdT, d_wq[2], 32, E, 2, "d_wdq")]

        def qpieces(d_w, nchunks, width):
            nsub = width // 2048 if width >= 2048 else 1
            subw = width // nsub
            for i in range(nchunks * nsub):
                c, si = divmod(i, nsub)
                yield i, c, si, subw, d_w[c * 128:(c + 1) * 128,
                                          si * subw:(si + 1) * subw]

        def quant_stageA(d_w, nchunks, width, j):
            for i, c, si, subw, piece in qpieces(d_w, nchunks, width):
                dma("gpsimd", lambda e, i=i, piece=piece, subw=subw: e.dma_start(
                    out=qwb[i % 2][:, 0:subw], in_=piece),
                    reads=[f"qwb{i % 2}"], writes=[f"qwb{i % 2}"], tag=f"qld{j}_{i}")
                dve(lambda e, i=i, j=j, subw=subw: e.reduce_sum(
                    qacc[j][:, i:i + 1], qwb[i % 2][:, 0:subw], axis=AX.X,
                    apply_absolute_value=True),
                    [f"qwb{i % 2}", f"qacc{j}"], [f"qacc{j}"], tag=f"qabs{j}_{i}")
                pr.last_writer[f"qwb{i % 2}"] = pr.ops[-1]["idx"]

        def quant_stageBC(d_w, d_wqt, nchunks, width, j, wqname):
            npc = sum(1 for _ in qpieces(d_w, nchunks, width))
            dve(lambda e, npc=npc, j=j: e.reduce_sum(qtot[j][:], qacc[j][:, 0:npc],
                                                     axis=AX.X),
                [f"qacc{j}", f"qtot{j}"], [f"qtot{j}"], tag=f"qtot{j}")
            dma("gpsimd", lambda e, j=j: e.dma_start(out=qrow[0:1, 0:128],
                                                     in_=qtot[j][:, 0:1]),
                reads=[f"qtot{j}", "qrow"], writes=["qrow"], tag=f"qgath{j}")
            dve(lambda e: e.reduce_sum(qrow[0:1, 128:129], qrow[0:1, 0:128], axis=AX.X),
                ["qrow"], ["qrow"], tag=f"qrsum{j}")
            dve(lambda e, j=j, nchunks=nchunks, width=width: e.tensor_single_scalar(
                qrow[0:1, 128:129], qrow[0:1, 128:129],
                1.0 / (nchunks * 128 * width), op=AL.mult), ["qrow"], ["qrow"])
            dve(lambda e: e.tensor_single_scalar(qrow[0:1, 128:129], qrow[0:1, 128:129],
                                                 1e-5, op=AL.max), ["qrow"], ["qrow"])
            dve(lambda e: e.tensor_scalar(qrow[0:1, 129:130], qrow[0:1, 128:129], -1.0,
                                          0.0, op0=AL.mult, op1=AL.add),
                ["qrow"], ["qrow"])
            dve(lambda e: e.reciprocal(qrow[0:1, 130:131], qrow[0:1, 128:129]),
                ["qrow"], ["qrow"])
            dve(lambda e: e.tensor_scalar(qrow[0:1, 131:259], ones[0:1, 0:128],
                                          qrow[0:1, 129:130], 0.0, op0=AL.mult,
                                          op1=AL.add), ["qrow", "ones"], ["qrow"],
                tag=f"qbc1_{j}")
            dve(lambda e: e.tensor_scalar(qrow[0:1, 259:387], ones[0:1, 0:128],
                                          qrow[0:1, 130:131], 0.0, op0=AL.mult,
                                          op1=AL.add), ["qrow", "ones"], ["qrow"],
                tag=f"qbc2_{j}")
            dma("gpsimd", lambda e, j=j: e.dma_start(out=swq[:, j:j + 1],
                                                     in_=qrow[0:1, 131:259]),
                reads=["qrow", "swq"], writes=["swq"], tag=f"qscat1_{j}")
            dma("gpsimd", lambda e, j=j: e.dma_start(out=qtot[j][:, 0:1],
                                                     in_=qrow[0:1, 259:387]),
                reads=["qrow", f"qtot{j}"], writes=[f"qtot{j}"], tag=f"qscat2_{j}")
            for i, c, si, subw, piece in qpieces(d_w, nchunks, width):
                dma("gpsimd", lambda e, i=i, piece=piece, subw=subw: e.dma_start(
                    out=qwb[i % 2][:, 0:subw], in_=piece),
                    reads=[f"qwb{i % 2}"], writes=[f"qwb{i % 2}"], tag=f"qld2{j}_{i}")
                dve(lambda e, i=i, j=j, subw=subw: e.tensor_scalar(
                    qwb[i % 2][:, 0:subw], qwb[i % 2][:, 0:subw], qtot[j][:], 0.0,
                    op0=AL.mult, op1=AL.add),
                    [f"qwb{i % 2}", f"qtot{j}"], [f"qwb{i % 2}"], tag=f"qmul{j}_{i}")
                dve(lambda e, i=i, subw=subw: e.tensor_single_scalar(
                    qpos[:, 0:subw], qwb[i % 2][:, 0:subw], 0.5, op=AL.is_gt),
                    [f"qwb{i % 2}", "qpos"], ["qpos"], tag=f"qpos{j}_{i}")
                dve(lambda e, i=i, subw=subw: e.scalar_tensor_tensor(
                    qwq[:, 0:subw], qwb[i % 2][:, 0:subw], -0.5, qpos[:, 0:subw],
                    op0=AL.is_lt, op1=AL.subtract),
                    [f"qwb{i % 2}", "qpos", "qwq"], ["qwq"], tag=f"qneg{j}_{i}")
                pr.last_writer[f"qwb{i % 2}"] = pr.ops[-1]["idx"]
                dma("gpsimd", lambda e, c=c, si=si, subw=subw, d_wqt=d_wqt: e.dma_start(
                    out=d_wqt[c, :, si * subw:(si + 1) * subw], in_=qwq[:, 0:subw]),
                    reads=["qwq"], writes=[wqname], tag=f"qst{j}_{c}_{si}")
                pr.last_writer["qwq"] = pr.ops[-1]["idx"]

        for (d_w, d_wqt, nchunks, width, j, wqname) in QSPECS:
            quant_stageA(d_w, nchunks, width, j)

        with contextlib.ExitStack() as p2:
            xc = [sb(f"xc{i}", [128, S], st=p2) for i in range(2)]
            xsq = sb("xsq", [128, 2048], st=p2)
            m_bc = sb("m_bc", [128, S], st=p2)
            rstd_bc = sb("rstd_bc", [128, S], st=p2)
            h32 = sb("h32", [128, S], st=p2)
            hhi = sb("hhi", [128, S], BF16, st=p2)
            hlo = sb("hlo", [128, S], BF16, st=p2)
            hsqb = sb("hsqb", [128, S], BF16, st=p2)
            wav32 = sb("wav32", [128, 128 * NCH], st=p2)
            wavp = [sb(f"wavp{p}", [128, 128 * NCH], BF16, st=p2) for p in range(2)]
            q32 = sb("q32", [128, TQ], st=p2)
            qhi = sb("qhi", [128, TQ], BF16, st=p2)
            qlo = sb("qlo", [128, TQ], BF16, st=p2)
            qsqb = sb("qsqb", [128, TQ], BF16, st=p2)

            for c in range(NCH):
                dma("sync", lambda e, c=c: e.dma_start(
                    out=wav32[:, c * 128:(c + 1) * 128], in_=d_wav[c, :, :]),
                    writes=["wav32"], tag=f"ldwav{c}")
            dve(lambda e: e.tensor_copy(wavp[0][:], wav32[:]), ["wav32"], ["wavh"])
            dve(lambda e: e.tensor_tensor(wavp[1][:], wav32[:], wavp[0][:],
                                          op=AL.subtract), ["wav32", "wavh"], ["wavl"])

            with contextlib.ExitStack() as p2a:
                st_ps = psum("st_ps", [128, 4096], st=p2a)
                for sh in range(2):
                    for c in range(NCH):
                        buf = xc[c % 2]
                        dma("sync", lambda e, sh=sh, c=c, buf=buf: e.dma_start(
                            out=buf[:, 0:2048],
                            in_=d_xT[c * 128:(c + 1) * 128, sh * 2048:(sh + 1) * 2048]),
                            reads=[f"xcr{c % 2}"], writes=[f"xc{c % 2}"],
                            tag=f"ldxs{sh}_{c}")
                        sc(lambda e, buf=buf: e.activation(xsq[:], buf[:, 0:2048],
                                                           AF.Square),
                           [f"xc{c % 2}", "xsqr"], ["xsq"], tag=f"sq{sh}_{c}")

                        def statmm(e, c=c, buf=buf):
                            last = None
                            for nt in range(4):
                                last = e.matmul(st_ps[:, nt * 512:(nt + 1) * 512], ones[:],
                                                buf[:, nt * 512:(nt + 1) * 512],
                                                start=(c == 0), stop=(c == NCH - 1))
                            for nt in range(4):
                                last = e.matmul(st_ps[:, 2048 + nt * 512:2048 + (nt + 1) * 512],
                                                ones[:], xsq[:, nt * 512:(nt + 1) * 512],
                                                start=(c == 0), stop=(c == NCH - 1))
                            return last
                        pe(statmm, [f"xc{c % 2}", "xsq", "ones", "st_ps"],
                           ["st_ps", f"xcr{c % 2}", "xsqr"], tag=f"stat{sh}_{c}")
                    sl = slice(sh * 2048, (sh + 1) * 2048)
                    dve(lambda e, sl=sl: e.tensor_single_scalar(
                        m_bc[:, sl], st_ps[:, 0:2048], 1.0 / E, op=AL.mult),
                        ["st_ps"], ["m_bc"], tag=f"mev{sh}")
                    dve(lambda e, sl=sl: e.tensor_tensor(
                        rstd_bc[:, sl], m_bc[:, sl], m_bc[:, sl], op=AL.mult),
                        ["m_bc"], ["m2"], tag=f"m2_{sh}")
                    dve(lambda e, sl=sl: e.scalar_tensor_tensor(
                        rstd_bc[:, sl], st_ps[:, 2048:4096], 1.0 / E, rstd_bc[:, sl],
                        op0=AL.mult, op1=AL.subtract),
                        ["st_ps", "m2"], ["var"], tag=f"var{sh}")
                    sc(lambda e, sl=sl: e.activation(rstd_bc[:, sl], rstd_bc[:, sl],
                                                     AF.Sqrt, bias=epscol[:]),
                       ["var", "epscol"], ["sstd"], tag=f"std{sh}")
                    dve(lambda e, sl=sl: e.reciprocal(rstd_bc[:, sl], rstd_bc[:, sl]),
                        ["sstd", "st_ps"], ["rstd_bc"], tag=f"rstd{sh}")
            pr.fence()

            with contextlib.ExitStack() as p2b:
                n_ps = psum("n_ps", [128, 2048], st=p2b)
                for c in range(NCH):
                    buf = xc[c % 2]
                    dma("sync", lambda e, c=c, buf=buf: e.dma_start(
                        out=buf[:], in_=d_xT[c * 128:(c + 1) * 128, :]),
                        reads=[f"xcr{c % 2}"], writes=[f"xc{c % 2}"], tag=f"ldxc{c}")
                    dve(lambda e, buf=buf: e.tensor_tensor(h32[:], buf[:], m_bc[:],
                                                           op=AL.subtract),
                        [f"xc{c % 2}", "m_bc", "h32r"], ["h32"], tag=f"h1_{c}")
                    pr.last_writer[f"xcr{c % 2}"] = pr.ops[-1]["idx"]
                    dve(lambda e: e.tensor_tensor(h32[:], h32[:], rstd_bc[:], op=AL.mult),
                        ["h32", "rstd_bc"], ["h32"])
                    dve(lambda e, c=c: e.tensor_scalar(h32[:], h32[:], ca1c[:, c:c + 1],
                                                       sac[:, c:c + 1], op0=AL.mult,
                                                       op1=AL.add),
                        ["h32", "ca1c", "sac"], ["h32"], tag=f"h3_{c}")
                    dve(lambda e: e.tensor_copy(hhi[:], h32[:]), ["h32", "hhir"], ["hhi"])
                    dve(lambda e: e.tensor_tensor(hlo[:], h32[:], hhi[:], op=AL.subtract),
                        ["h32", "hhi", "hlor"], ["hlo"])
                    dma("scalar", lambda e, c=c: e.dma_start(
                        out=d_h[0][c * 128:(c + 1) * 128, :], in_=hhi[:]),
                        reads=["hhi"], writes=["d_h0"], tag=f"sph0_{c}")
                    pr.last_writer["hhir"] = pr.ops[-1]["idx"]
                    dma("scalar", lambda e, c=c: e.dma_start(
                        out=d_h[1][c * 128:(c + 1) * 128, :], in_=hlo[:]),
                        reads=["hlo"], writes=["d_h1"], tag=f"sph1_{c}")
                    pr.last_writer["hlor"] = pr.ops[-1]["idx"]
                    sc(lambda e: e.activation(hsqb[:], h32[:], AF.Square),
                       ["h32", "hsqbr"], ["hsqb"], tag=f"hsq{c}")
                    for hh, lhs in ((0, onesAb), (1, onesBb)):
                        for rnd in range(2):
                            def knmm(e, lhs=lhs, rnd=rnd):
                                last = None
                                for nt in range(4):
                                    last = e.matmul(
                                        n_ps[:, nt * 512:(nt + 1) * 512], lhs[:],
                                        hsqb[:, rnd * 2048 + nt * 512:
                                             rnd * 2048 + (nt + 1) * 512],
                                        start=True, stop=True)
                                return last
                            pe(knmm, ["hsqb", "onesAb", "onesBb", "n_ps"], ["n_ps"],
                               tag=f"knmm{c}_{hh}_{rnd}")
                            tgt = mksq if rnd == 0 else xsq
                            tcol = 2 * c + hh if rnd == 0 else 0
                            dve(lambda e, tgt=tgt, tcol=tcol: e.reduce_max(
                                tgt[:, tcol:tcol + 1], n_ps[:], axis=AX.X),
                                ["n_ps", "xsq" if rnd == 1 else "mksq"],
                                [f"krm{c}_{hh}_{rnd}", "mksq" if rnd == 0 else "xsq"],
                                tag=f"krm{c}_{hh}_{rnd}")
                        dve(lambda e, c=c, hh=hh: e.tensor_tensor(
                            mksq[:, 2 * c + hh:2 * c + hh + 1],
                            mksq[:, 2 * c + hh:2 * c + hh + 1], xsq[:, 0:1], op=AL.max),
                            [f"krm{c}_{hh}_0", f"krm{c}_{hh}_1", "xsq"], ["mksq"],
                            tag=f"kcmb{c}_{hh}")
                    pr.last_writer["hsqbr"] = pr.ops[-1]["idx"]

                    def qmm(e, c=c, buf=buf):
                        last = None
                        for pi, (lp, rp) in enumerate(((0, 0), (0, 1), (1, 0))):
                            rt = hhi if rp == 0 else hlo
                            for nt in range(4):
                                last = e.matmul(
                                    n_ps[:, nt * 512:(nt + 1) * 512],
                                    wavp[lp][:, c * 128:(c + 1) * 128],
                                    rt[:, nt * 512:(nt + 1) * 512],
                                    start=(pi == 0), stop=(pi == 2))
                        return last
                    pe(qmm, ["hhi", "hlo", "wavh", "wavl", "n_ps"], ["n_ps"],
                       tag=f"qmm{c}")
                    dve(lambda e: e.tensor_copy(q32[:], n_ps[:]),
                        ["n_ps", "q32r"], ["q32"], tag=f"qev{c}")
                    dve(lambda e: e.tensor_copy(qhi[:], q32[:]), ["q32", "qhir"], ["qhi"])
                    dve(lambda e: e.tensor_tensor(qlo[:], q32[:], qhi[:], op=AL.subtract),
                        ["q32", "qhi", "qlor"], ["qlo"])
                    dma("scalar", lambda e, c=c: e.dma_start(
                        out=d_q[0][c * 128:(c + 1) * 128, :], in_=qhi[:]),
                        reads=["qhi"], writes=["d_q0"], tag=f"spq0_{c}")
                    pr.last_writer["qhir"] = pr.ops[-1]["idx"]
                    dma("scalar", lambda e, c=c: e.dma_start(
                        out=d_q[1][c * 128:(c + 1) * 128, :], in_=qlo[:]),
                        reads=["qlo"], writes=["d_q1"], tag=f"spq1_{c}")
                    pr.last_writer["qlor"] = pr.ops[-1]["idx"]
                    sc(lambda e: e.activation(qsqb[:], q32[:], AF.Square),
                       ["q32", "qsqbr"], ["qsqb"], tag=f"qsq{c}")
                    pr.last_writer["q32r"] = pr.ops[-1]["idx"]
                    for hh, lhs in ((0, onesAb), (1, onesBb)):
                        def qnmm(e, lhs=lhs):
                            last = None
                            for nt in range(4):
                                last = e.matmul(n_ps[:, nt * 512:(nt + 1) * 512], lhs[:],
                                                qsqb[:, nt * 512:(nt + 1) * 512],
                                                start=True, stop=True)
                            return last
                        pe(qnmm, ["qsqb", "onesAb", "onesBb", "n_ps"], ["n_ps"],
                           tag=f"qnmm{c}_{hh}")
                        dve(lambda e, c=c, hh=hh: e.reduce_max(
                            mqsq[:, 2 * c + hh:2 * c + hh + 1], n_ps[:], axis=AX.X),
                            ["n_ps", "mqsq"], ["mqsq", f"qrm{c}_{hh}"],
                            tag=f"qrm{c}_{hh}")
                    pr.last_writer["qsqbr"] = pr.ops[-1]["idx"]
            sc(lambda e: e.activation(mqsq[:], mqsq[:], AF.Sqrt), ["mqsq"], ["mq_s"])
            sc(lambda e: e.activation(mksq[:], mksq[:], AF.Sqrt), ["mksq"], ["mk_s"])
            dve(lambda e: e.tensor_tensor(negC[:], mqsq[:], mksq[:], op=AL.mult),
                ["mq_s", "mk_s"], ["negC0"])
            dve(lambda e: e.tensor_single_scalar(negC[:], negC[:], -0.1275, op=AL.mult),
                ["negC0"], ["negC"])
            for pp in range(H // 2):
                dve(lambda e, pp=pp: e.tensor_tensor(
                    negCp[:, pp:pp + 1], negC[:, 2 * pp:2 * pp + 1],
                    negC[:, 2 * pp + 1:2 * pp + 2], op=AL.min),
                    ["negC", "negCp"], ["negCp"], tag=f"negCp{pp}")
            pr.fence()

        with contextlib.ExitStack() as p3:
            wvp = [sb(f"wvp{p}", [128, NCH * E], BF16, st=p3) for p in range(2)]
            wv32 = sb("wv32", [128, E], st=p3)
            hsl = [[sb(f"hsl{p}_{i}", [128, NCH * 512], BF16, st=p3) for i in range(2)]
                   for p in range(2)]
            v32p = [sb(f"v32_{i}", [128, E], st=p3) for i in range(2)]
            vaug = [sb(f"vaug{p}", [128, H * 65], BF16, st=p3) for p in range(2)]
            vps = [psum(f"vps{i}", [128, 1024], st=p3) for i in range(2)]

            dve(lambda e: e.tensor_copy(
                vaug[0][:].rearrange("p (h w) -> p h w", w=65)[:, :, 64:65],
                ones[:, 0:16].rearrange("p (h one) -> p h one", one=1)),
                ["ones"], ["vaug0c"], tag="vaug_ones")
            dve(lambda e: e.tensor_scalar(
                vaug[1][:].rearrange("p (h w) -> p h w", w=65)[:, :, 64:65],
                ones[:, 0:16].rearrange("p (h one) -> p h one", one=1), 0.0, 0.0,
                op0=AL.mult, op1=AL.add), ["ones"], ["vaug1c"], tag="vaug_zeros")
            for c in range(NCH):
                dma("sync", lambda e, c=c: e.dma_start(
                    out=wv32[:], in_=d_wvT[c * 128:(c + 1) * 128, :]),
                    reads=["wv32r"], writes=["wv32"], tag=f"ldwv{c}")
                dve(lambda e, c=c: e.tensor_copy(wvp[0][:, c * E:(c + 1) * E], wv32[:]),
                    ["wv32"], ["wvph"], tag=f"wvh{c}")
                dve(lambda e, c=c: e.tensor_tensor(wvp[1][:, c * E:(c + 1) * E], wv32[:],
                                                   wvp[0][:, c * E:(c + 1) * E],
                                                   op=AL.subtract),
                    ["wv32", "wvph"], ["wvpl"], tag=f"wvl{c}")
                pr.last_writer["wv32r"] = pr.ops[-1]["idx"]
            for kc in range(KC):
                par = kc % 2
                gpar3 = (kc // 4) % 2
                if kc % 4 == 0:
                    for c in range(NCH):
                        for pl in range(2):
                            dma("sync", lambda e, kc=kc, c=c, pl=pl, gpar3=gpar3:
                                e.dma_start(
                                    out=hsl[pl][gpar3][:, c * 512:(c + 1) * 512],
                                    in_=d_h[pl][c * 128:(c + 1) * 128,
                                                kc * 128:(kc + 4) * 128]),
                                reads=[f"hsl{pl}{gpar3}", "d_h0", "d_h1"],
                                writes=[f"hsl{pl}{gpar3}"], tag=f"ldh{kc}_{c}_{pl}")
                koff = (kc % 4) * 128

                def vmm(e, gpar3=gpar3, koff=koff, par=par):
                    last = None
                    for oh in range(2):
                        for pi, (lp, rp) in enumerate(((0, 0), (0, 1), (1, 0))):
                            for c in range(NCH):
                                last = e.matmul(
                                    vps[par][:, oh * 512:(oh + 1) * 512],
                                    hsl[lp][gpar3][:, c * 512 + koff:c * 512 + koff + 128],
                                    wvp[rp][:, c * E + oh * 512:c * E + (oh + 1) * 512],
                                    start=(pi == 0 and c == 0),
                                    stop=(pi == 2 and c == NCH - 1))
                    return last
                pe(vmm, [f"hsl0{gpar3}", f"hsl1{gpar3}", "wvph", "wvpl", f"vps{par}"],
                   [f"vps{par}"], tag=f"vmm{kc}")
                dve(lambda e, par=par: e.tensor_tensor(v32p[par][:], vps[par][:],
                                                       wvb_bc[:], op=AL.add),
                    [f"vps{par}", "wvb_bc", f"v32_{par}"], [f"v32_{par}"],
                    tag=f"vb{kc}")

                dve(lambda e, par=par: e.tensor_copy(
                    vaug[0][:].rearrange("p (h w) -> p h w", w=65)[:, :, 0:64],
                    v32p[par][:].rearrange("p (h w) -> p h w", w=64)),
                    [f"v32_{par}", "vaug0r", "vaug0c"], ["vaug0"], tag=f"vhi{kc}")
                dve(lambda e, par=par: e.tensor_tensor(
                    vaug[1][:].rearrange("p (h w) -> p h w", w=65)[:, :, 0:64],
                    v32p[par][:].rearrange("p (h w) -> p h w", w=64),
                    vaug[0][:].rearrange("p (h w) -> p h w", w=65)[:, :, 0:64],
                    op=AL.subtract),
                    [f"v32_{par}", "vaug0", "vaug1r", "vaug1c"], ["vaug1"],
                    tag=f"vlo{kc}")
                dma("scalar", lambda e, kc=kc: e.dma_start(
                    out=d_v[0][kc * 128:(kc + 1) * 128, :, :]
                    .rearrange("p h w -> p (h w)"),
                    in_=vaug[0][:]), reads=["vaug0"], writes=["d_v0"], tag=f"spv0_{kc}")
                pr.last_writer["vaug0r"] = pr.ops[-1]["idx"]
                dma("scalar", lambda e, kc=kc: e.dma_start(
                    out=d_v[1][kc * 128:(kc + 1) * 128, :, :]
                    .rearrange("p h w -> p (h w)"),
                    in_=vaug[1][:]), reads=["vaug1"], writes=["d_v1"], tag=f"spv1_{kc}")
                pr.last_writer["vaug1r"] = pr.ops[-1]["idx"]
            pr.fence()
        for (d_w, d_wqt, nchunks, width, j, wqname) in QSPECS:
            quant_stageBC(d_w, d_wqt, nchunks, width, j, wqname)

        with contextlib.ExitStack() as p4:
            kg = [[sb(f"kg{p}_{t}", [128, S], BF16, st=p4) for t in range(2)]
                  for p in range(2)]
            qg = [[sb(f"qg{p}_{t}", [128, TQ], BF16, st=p4) for t in range(2)]
                  for p in range(2)]
            vg = [sb(f"vg{p}", [128, KC * 260], BF16, st=p4) for p in range(2)]
            p32 = [sb(f"p32_{i}", [128, 1024], st=p4) for i in range(4)]
            phi = [sb(f"phi{i}", [128, 1024], BF16, st=p4) for i in range(4)]
            plo = [sb(f"plo{i}", [128, 1024], BF16, st=p4) for i in range(4)]
            stg = sb("stg", [64, 512], st=p4)
            sumr = sb("sumr", [1, 512], st=p4)
            rb_sb = sb("rb_sb", [64, 512], st=p4)
            oghi = sb("oghi", [64, 512], BF16, st=p4)
            oglo = sb("oglo", [64, 512], BF16, st=p4)
            s_ps = [psum(f"s_ps{i}", [128, 1024], st=p4) for i in range(2)]
            o_psA = psum("o_psA", [65, 512], st=p4)
            o_psB = psum("o_psB", [65, 512], st=p4)
            rb_ps = psum("rb_ps", [64, 512], st=p4)

            for g in range(4):
                for t2 in range(2):
                    for pl in range(2):
                        dma("sync", lambda e, g=g, t2=t2, pl=pl: e.dma_start(
                            out=kg[pl][t2][:],
                            in_=d_h[pl][(g * 2 + t2) * 128:(g * 2 + t2 + 1) * 128, :]),
                            reads=[f"kg{pl}{t2}", "d_h0", "d_h1"],
                            writes=[f"kg{pl}{t2}"], tag=f"ldkg{g}_{t2}_{pl}")
                        dma("sync", lambda e, g=g, t2=t2, pl=pl: e.dma_start(
                            out=qg[pl][t2][:],
                            in_=d_q[pl][(g * 2 + t2) * 128:(g * 2 + t2 + 1) * 128, :]),
                            reads=[f"qg{pl}{t2}", "d_q0", "d_q1"],
                            writes=[f"qg{pl}{t2}"], tag=f"ldqg{g}_{t2}_{pl}")
                for pl in range(2):
                    for kc in range(KC):
                        dma("sync", lambda e, g=g, pl=pl, kc=kc: e.dma_start(
                            out=vg[pl][:, kc * 260:(kc + 1) * 260],
                            in_=d_v[pl][kc * 128:(kc + 1) * 128, g * 4:(g + 1) * 4, :]
                            .rearrange("p h w -> p (h w)")),
                            reads=[f"vg{pl}", "d_v0", "d_v1"], writes=[f"vg{pl}"],
                            tag=f"ldvg{g}_{pl}_{kc}")
                for p2 in range(2):
                    pair = g * 2 + p2

                    def smm(e, p2, q0, kc, par):
                        last = None
                        for pi, (lp, rp) in enumerate(((0, 0), (0, 1), (1, 0))):
                            for hh in range(2):
                                base = 64 * hh
                                last = e.matmul(
                                    s_ps[par][:, hh * 512:(hh + 1) * 512],
                                    kg[lp][p2][base:base + 64, kc * 128:(kc + 1) * 128],
                                    qg[rp][p2][base:base + 64, q0:q0 + 512],
                                    start=(pi == 0), stop=(pi == 2),
                                    tile_position=(base, 0))
                        return last

                    def avmm(e, p2, kc, par):
                        last = None
                        for pi, (lp, rp) in enumerate(((0, 0), (0, 1), (1, 0))):
                            rt = (phi if rp == 0 else plo)[par]
                            for hh in range(2):
                                hcol = kc * 260 + (2 * p2 + hh) * 65
                                ops_ = o_psA if hh == 0 else o_psB
                                last = e.matmul(
                                    ops_[:], vg[lp][:, hcol:hcol + 65],
                                    rt[:, hh * 512:(hh + 1) * 512],
                                    start=(pi == 0 and kc == 0),
                                    stop=(pi == 2 and kc == KC - 1))
                        return last

                    def emit_scores(thq, kc, i):
                        par = i % 2
                        pp4 = i % 4
                        q0 = thq * 512
                        pe(lambda e, q0=q0, kc=kc, par=par, f=smm, p2=p2:
                           f(e, p2, q0, kc, par),
                           [f"kg0{p2}", f"kg1{p2}", f"qg0{p2}", f"qg1{p2}", f"sS{par}"],
                           [f"sS{par}"], tag=f"smm{pair}_{thq}_{kc}")
                        sc(lambda e, par=par, pp4=pp4: e.activation(
                            p32[pp4][:], s_ps[par][:], AF.Exp,
                            bias=negCp[:, pair:pair + 1], scale=0.125),
                           [f"sS{par}", "negCp", f"p32_{pp4}"], [f"p32_{pp4}"],
                           tag=f"exp{pair}_{thq}_{kc}")
                        sc(lambda e, pp4=pp4: e.activation(phi[pp4][:], p32[pp4][:],
                                                           AF.Copy),
                           [f"p32_{pp4}", f"phi{pp4}"], [f"phi{pp4}"],
                           tag=f"phic{pair}_{thq}_{kc}")
                        dve(lambda e, pp4=pp4: e.tensor_tensor(
                            plo[pp4][:], p32[pp4][:], phi[pp4][:], op=AL.subtract),
                            [f"p32_{pp4}", f"phi{pp4}", f"plo{pp4}"], [f"plo{pp4}"],
                            tag=f"plo{pair}_{thq}_{kc}")

                    def emit_av(thq, kc, i):
                        pp4 = i % 4
                        pe(lambda e, kc=kc, pp4=pp4, f=avmm, p2=p2: f(e, p2, kc, pp4),
                           ["vg0", "vg1", f"phi{pp4}", f"plo{pp4}", "oA", "oB"],
                           ["oA", "oB", f"phi{pp4}", f"plo{pp4}"],
                           tag=f"avmm{pair}_{thq}_{kc}")

                    def emit_evac(thq):
                        q0 = thq * 512
                        for hh in range(2):
                            head = 2 * pair + hh
                            ops_ = o_psA if hh == 0 else o_psB
                            dve(lambda e, ops_=ops_: e.tensor_single_scalar(
                                sumr[:], ops_[64:65, :], 1e-30, op=AL.max),
                                ["oA", "oB", "sumr"], ["sumr"],
                                tag=f"sumc{pair}_{thq}_{hh}")
                            dve(lambda e: e.reciprocal(sumr[:], sumr[:]),
                                ["sumr"], ["sumr"])
                            pe(lambda e: e.matmul(rb_ps[:], ones[0:1, 0:64], sumr[:],
                                                  start=True, stop=True),
                               ["sumr", "ones", "rb_ps"], ["rb_ps"],
                               tag=f"rbmm{pair}_{thq}_{hh}")
                            dve(lambda e: e.tensor_copy(rb_sb[:], rb_ps[:]),
                                ["rb_ps", "rb_sb"], ["rb_sb"],
                                tag=f"rbev{pair}_{thq}_{hh}")
                            dve(lambda e, ops_=ops_: e.tensor_tensor(
                                stg[:], ops_[0:64, :], rb_sb[:], op=AL.mult),
                                ["oA", "oB", "rb_sb", "stg"], ["stg"],
                                tag=f"onrm{pair}_{thq}_{hh}")
                            dve(lambda e: e.tensor_copy(oghi[:], stg[:]),
                                ["stg", "oghi"], ["oghi"])
                            dve(lambda e: e.tensor_tensor(oglo[:], stg[:], oghi[:],
                                                          op=AL.subtract),
                                ["stg", "oghi", "oglo"], ["oglo"])
                            dma("scalar", lambda e, head=head, q0=q0: e.dma_start(
                                out=d_o[0][head * 64:(head + 1) * 64, q0:q0 + 512],
                                in_=oghi[:]), reads=["oghi"], writes=["d_o0"],
                                tag=f"spo0_{pair}_{thq}_{hh}")
                            pr.last_writer["oghi"] = pr.ops[-1]["idx"]
                            dma("scalar", lambda e, head=head, q0=q0: e.dma_start(
                                out=d_o[1][head * 64:(head + 1) * 64, q0:q0 + 512],
                                in_=oglo[:]), reads=["oglo"], writes=["d_o1"],
                                tag=f"spo1_{pair}_{thq}_{hh}")
                            pr.last_writer["oglo"] = pr.ops[-1]["idx"]

                    flat = [(thq, kc) for thq in range(4) for kc in range(KC)]
                    LEAD = 2
                    emitted_av = 0
                    for i in range(len(flat) + LEAD):
                        if i < len(flat):
                            emit_scores(*flat[i], i)
                        if i >= LEAD:
                            j = i - LEAD
                            emit_av(*flat[j], j)
                            if flat[j][1] == KC - 1:
                                emit_evac(flat[j][0])
            pr.fence()
        qs.close()

        p58 = stk.enter_context(contextlib.ExitStack())
        rescale2 = sb("rescale2", [128, 16], st=p58)
        kq2T = sb("kq2T", [128, 16 * E], BF16, st=p58)
        with contextlib.ExitStack() as p5:
            owp = [sb(f"owp{p}", [128, NCH * E], BF16, st=p5) for p in range(2)]
            ow32 = sb("ow32", [128, E], st=p5)
            osl = [[sb(f"osl{p}_{i}", [128, NCH * 512], BF16, st=p5) for i in range(2)]
                   for p in range(2)]
            xo_t = [sb(f"xo_t{i}", [128, E], st=p5) for i in range(2)]
            xp_t = [sb(f"xp_t{i}", [128, E], st=p5) for i in range(2)]
            h2 = [sb(f"h2_{i}", [128, E], st=p5) for i in range(2)]
            sqs = sb("sqs", [128, E], st=p5)
            kq2 = [sb(f"kq2_{i}", [128, E], BF16, st=p5) for i in range(2)]
            sA = [sb(f"sA{i}", [128, 1], st=p5) for i in range(2)]
            sB = [sb(f"sB{i}", [128, 1], st=p5) for i in range(2)]
            sM = [sb(f"sM{i}", [128, 1], st=p5) for i in range(2)]
            sR = [sb(f"sR{i}", [128, 1], st=p5) for i in range(2)]
            o_ps5 = [psum(f"o_ps5{i}", [128, 1024], st=p5) for i in range(2)]
            t_ps5 = psum("t_ps5", [128, 1024], BF16, st=p5)

            for c in range(NCH):
                dma("sync", lambda e, c=c: e.dma_start(
                    out=ow32[:], in_=d_owT[c * 128:(c + 1) * 128, :]),
                    reads=["ow32r"], writes=["ow32"], tag=f"ldow{c}")
                dve(lambda e, c=c: e.tensor_copy(owp[0][:, c * E:(c + 1) * E], ow32[:]),
                    ["ow32"], ["owph"], tag=f"owh{c}")
                dve(lambda e, c=c: e.tensor_tensor(owp[1][:, c * E:(c + 1) * E], ow32[:],
                                                   owp[0][:, c * E:(c + 1) * E],
                                                   op=AL.subtract),
                    ["ow32", "owph"], ["owpl"], tag=f"owl{c}")
                pr.last_writer["ow32r"] = pr.ops[-1]["idx"]
            for t in range(16):
                par = t % 2
                tg = t // 4
                if t % 4 == 0:
                    gpar = tg % 2
                    for pl in range(2):
                        for c in range(NCH):
                            dma("sync", lambda e, tg=tg, pl=pl, c=c, gpar=gpar: e.dma_start(
                                out=osl[pl][gpar][:, c * 512:(c + 1) * 512],
                                in_=d_o[pl][c * 128:(c + 1) * 128,
                                            tg * 512:(tg + 1) * 512]),
                                reads=[f"osl{pl}{gpar}", "d_o0", "d_o1"],
                                writes=[f"osl{pl}{gpar}"], tag=f"ldosl{tg}_{pl}_{c}")
                gpar = tg % 2
                toff = (t % 4) * 128
                dma("sync", lambda e, t=t, par=par: e.dma_start(
                    out=xo_t[par][:], in_=d_xo[t * 128:(t + 1) * 128, :]),
                    reads=[f"xo_t{par}"], writes=[f"xo_t{par}"], tag=f"ldxo{t}")

                def omm(e, gpar=gpar, toff=toff, par=par):
                    last = None
                    for pi, (lp, rp) in enumerate(((0, 0), (0, 1), (1, 0))):
                        for c in range(NCH):
                            for oh in range(2):
                                last = e.matmul(
                                    o_ps5[par][:, oh * 512:(oh + 1) * 512],
                                    osl[lp][gpar][:, c * 512 + toff:c * 512 + toff + 128],
                                    owp[rp][:, c * E + oh * 512:c * E + (oh + 1) * 512],
                                    start=(pi == 0 and c == 0),
                                    stop=(pi == 2 and c == NCH - 1))
                    return last
                pe(omm, [f"osl0{gpar}", f"osl1{gpar}", "owph", "owpl", f"o_ps5{par}"],
                   [f"o_ps5{par}"], tag=f"omm{t}")
                dve(lambda e, par=par: e.tensor_tensor(xp_t[par][:], o_ps5[par][:],
                                                       owb_bc[:], op=AL.add),
                    [f"o_ps5{par}", "owb_bc", f"xp_t{par}"], [f"xp_t{par}"], tag=f"xp1_{t}")
                dve(lambda e, par=par: e.tensor_tensor(xp_t[par][:], xp_t[par][:],
                                                       ga_bc[:], op=AL.mult),
                    [f"xp_t{par}", "ga_bc"], [f"xp_t{par}"])
                dve(lambda e, par=par: e.tensor_tensor(xp_t[par][:], xp_t[par][:],
                                                       xo_t[par][:], op=AL.add),
                    [f"xp_t{par}", f"xo_t{par}"], [f"xp_t{par}"])
                dma("scalar", lambda e, t=t, par=par: e.dma_start(
                    out=d_xp[t * 128:(t + 1) * 128, :], in_=xp_t[par][:]),
                    reads=[f"xp_t{par}"], writes=["d_xp"], tag=f"spxp{t}")
                dve(lambda e, par=par: e.reduce_sum(sA[par][:], xp_t[par][:], axis=AX.X),
                    [f"xp_t{par}", f"sA{par}"], [f"sA{par}"], tag=f"ln2s{t}")
                sc(lambda e, par=par: e.activation(sqs[:], xp_t[par][:], AF.Square,
                                                   accum_out=sB[par][:]),
                   [f"xp_t{par}", f"sB{par}", "sqs"], [f"sB{par}", "sqs"], tag=f"ln2q{t}")
                dve(lambda e, par=par: e.tensor_single_scalar(sM[par][:], sA[par][:],
                                                              1.0 / E, op=AL.mult),
                    [f"sA{par}", f"sM{par}"], [f"sM{par}"])
                dve(lambda e, par=par: e.tensor_tensor(sR[par][:], sM[par][:], sM[par][:],
                                                       op=AL.mult),
                    [f"sM{par}", f"sR{par}"], [f"sR{par}"])
                dve(lambda e, par=par: e.scalar_tensor_tensor(sR[par][:], sB[par][:],
                                                              1.0 / E, sR[par][:],
                                                              op0=AL.mult,
                                                              op1=AL.subtract),
                    [f"sB{par}", f"sR{par}"], [f"sR{par}"])
                sc(lambda e, par=par: e.activation(sR[par][:], sR[par][:], AF.Sqrt,
                                                   bias=epscol[:]),
                   [f"sR{par}", "epscol"], [f"sR{par}"], tag=f"ln2std{t}")
                dve(lambda e, par=par: e.reciprocal(sR[par][:], sR[par][:]),
                    [f"sR{par}"], [f"sR{par}"])
                dve(lambda e, par=par: e.tensor_scalar(h2[par][:], xp_t[par][:], sM[par][:],
                                                       sR[par][:], op0=AL.subtract,
                                                       op1=AL.mult),
                    [f"xp_t{par}", f"sM{par}", f"sR{par}", f"h2_{par}"], [f"h2_{par}"],
                    tag=f"ln2n{t}")
                dve(lambda e, par=par: e.tensor_tensor(h2[par][:], h2[par][:], cb1_bc[:],
                                                       op=AL.mult),
                    [f"h2_{par}", "cb1_bc"], [f"h2_{par}"])
                dve(lambda e, par=par: e.tensor_tensor(h2[par][:], h2[par][:], sbb_bc[:],
                                                       op=AL.add),
                    [f"h2_{par}", "sbb_bc"], [f"h2_{par}"])
                dve(lambda e, par=par: e.reduce_max(sA[par][:], h2[par][:], axis=AX.X,
                                                    apply_absolute_value=True),
                    [f"h2_{par}", f"sA{par}"], [f"sA{par}"], tag=f"am2_{t}")
                dve(lambda e, par=par: e.tensor_single_scalar(sA[par][:], sA[par][:], 1e-5,
                                                              op=AL.max),
                    [f"sA{par}"], [f"sA{par}"])
                dve(lambda e, t=t, par=par: e.tensor_single_scalar(
                    rescale2[:, t:t + 1], sA[par][:], 1.0 / 127.0, op=AL.mult),
                    [f"sA{par}"], ["rescale2"], tag=f"rsc2_{t}")
                dve(lambda e, par=par: e.reciprocal(sB[par][:], sA[par][:]),
                    [f"sA{par}", f"sB{par}"], [f"sB{par}"])
                dve(lambda e, par=par: e.tensor_single_scalar(sB[par][:], sB[par][:], 127.0,
                                                              op=AL.mult),
                    [f"sB{par}"], [f"sB{par}"])
                dve(lambda e, par=par: e.tensor_scalar(h2[par][:], h2[par][:], sB[par][:],
                                                       0.0, op0=AL.mult, op1=AL.add),
                    [f"h2_{par}", f"sB{par}"], [f"h2_{par}"])
                dve(lambda e, par=par: e.tensor_scalar(kq2[par][:], h2[par][:], magicC[:],
                                                       magicC[:], op0=AL.add,
                                                       op1=AL.subtract),
                    [f"h2_{par}", "magicC", f"kq2_{par}"], [f"kq2_{par}"],
                    tag=f"round2_{t}")

                def tmm(e, par=par):
                    last = None
                    for c in range(NCH):
                        last = e.transpose(t_ps5[:, c * 128:(c + 1) * 128],
                                           kq2[par][:, c * 128:(c + 1) * 128], identb[:])
                    return last
                pe(tmm, [f"kq2_{par}", "identb", "t_ps5"], ["t_ps5"], tag=f"kq2T{t}")
                dve(lambda e, t=t: e.tensor_copy(kq2T[:, t * E:(t + 1) * E], t_ps5[:]),
                    ["t_ps5"], ["kq2T"], tag=f"kq2Tev{t}")
            pr.fence()

        TB = 2
        with contextlib.ExitStack() as p8:
            wgs = [sb(f"wgs{i}", [128, NCH * 512], BF16, st=p8) for i in range(2)]
            wus = [sb(f"wus{i}", [128, NCH * 512], BF16, st=p8) for i in range(2)]
            wds = [sb(f"wds{i}", [128, 8 * 512], BF16, st=p8) for i in range(2)]
            gu = [sb(f"gu{i}", [128, INNER], st=p8) for i in range(TB)]
            sil = [sb(f"sil{i}", [128, 512], st=p8) for i in range(2)]
            upv = [sb(f"upv{i}", [128, 512], st=p8) for i in range(2)]
            kq3 = sb("kq3", [128, INNER], BF16, st=p8)
            kq3T = [sb(f"kq3T{i}", [128, INNER], BF16, st=p8) for i in range(TB)]
            xp8 = sb("xp8", [128, E], st=p8)
            yt = [sb(f"yt{i}", [128, E], st=p8) for i in range(TB)]
            a1 = sb("a1", [128, 1], st=p8)
            a2 = sb("a2", [128, 1], st=p8)
            gsc = sb("gsc", [128, 16], st=p8)
            usc = sb("usc", [128, 16], st=p8)
            dsc = sb("dsc", [128, 16], st=p8)
            g_ps = [psum("g_ps0", [128, 512], st=p8)]
            u_ps = [psum("u_ps0", [128, 512], st=p8)]
            dn_ps = psum("dn_ps", [128, 2048], st=p8)
            t_ps8_t = psum("t_ps8", [128, 2048], BF16, st=p8)
            t_ps8 = t_ps8_t[:]

            dve(lambda e: e.tensor_scalar(gsc[:], rescale2[:], swq[:, 0:1], 0.0,
                                          op0=AL.mult, op1=AL.add),
                ["rescale2", "swq"], ["gsc"])
            dve(lambda e: e.tensor_scalar(usc[:], rescale2[:], swq[:, 1:2], 0.0,
                                          op0=AL.mult, op1=AL.add),
                ["rescale2", "swq"], ["usc"])
            for tb in range(16 // TB):
                for ot in range(8):
                    wpar = ot % 2
                    dma("sync", lambda e, ot=ot, wpar=wpar: e.dma_start(
                        out=wgs[wpar][:].rearrange("p (c w) -> p c w", c=NCH),
                        in_=d_wq[0][:, :, ot * 512:(ot + 1) * 512]
                        .rearrange("c p w -> p c w")),
                        reads=[f"wgsr{wpar}", "d_wgq"], writes=[f"wgs{wpar}"],
                        tag=f"ldwg{tb}_{ot}")
                    dma("gpsimd", lambda e, ot=ot, wpar=wpar: e.dma_start(
                        out=wus[wpar][:].rearrange("p (c w) -> p c w", c=NCH),
                        in_=d_wq[1][:, :, ot * 512:(ot + 1) * 512]
                        .rearrange("c p w -> p c w")),
                        reads=[f"wusr{wpar}", "d_wuq"], writes=[f"wus{wpar}"],
                        tag=f"ldwu{tb}_{ot}")
                    for ti in range(TB):
                        t = tb * TB + ti
                        gp_ = 0

                        def gumm(e, t=t, wpar=wpar, gp_=gp_):
                            last = None
                            for c in range(NCH):
                                lhs = kq2T[:, t * E + c * 128:t * E + (c + 1) * 128]
                                e.matmul(g_ps[gp_][:], lhs,
                                         wgs[wpar][:, c * 512:(c + 1) * 512],
                                         start=(c == 0), stop=(c == NCH - 1))
                                last = e.matmul(u_ps[gp_][:], lhs,
                                                wus[wpar][:, c * 512:(c + 1) * 512],
                                                start=(c == 0), stop=(c == NCH - 1))
                            return last
                        pe(gumm, ["kq2T", f"wgs{wpar}", f"wus{wpar}", f"g_ps{gp_}",
                                  f"u_ps{gp_}"],
                           [f"g_ps{gp_}", f"u_ps{gp_}"], tag=f"gumm{tb}_{ot}_{ti}")
                        sc(lambda e, t=t, gp_=gp_: e.activation(
                            sil[gp_][:], g_ps[gp_][:], AF.Silu, scale=gsc[:, t:t + 1]),
                           [f"g_ps{gp_}", "gsc", f"sil{gp_}"], [f"sil{gp_}"],
                           tag=f"silu{tb}_{ot}_{ti}")
                        dve(lambda e, t=t, gp_=gp_: e.tensor_scalar(
                            upv[gp_][:], u_ps[gp_][:], usc[:, t:t + 1], 0.0,
                            op0=AL.mult, op1=AL.add),
                            [f"u_ps{gp_}", "usc", f"upv{gp_}"], [f"upv{gp_}"],
                            tag=f"us{tb}_{ot}_{ti}")
                        pr.last_writer[f"wgsr{wpar}"] = pr.ops[-1]["idx"]
                        pr.last_writer[f"wusr{wpar}"] = pr.ops[-1]["idx"]
                        dve(lambda e, ti=ti, ot=ot, gp_=gp_: e.tensor_tensor(
                            gu[ti][:, ot * 512:(ot + 1) * 512], sil[gp_][:], upv[gp_][:],
                            op=AL.mult), [f"sil{gp_}", f"upv{gp_}", f"gur{ti}"],
                            [f"gu{ti}_{ot}", f"sil{gp_}", f"upv{gp_}"],
                            tag=f"gu{tb}_{ot}_{ti}")
                for ti in range(TB):
                    t = tb * TB + ti
                    gu_all = [f"gu{ti}_{ot}" for ot in range(8)]
                    dve(lambda e, ti=ti: e.reduce_max(a1[:], gu[ti][:], axis=AX.X,
                                                      apply_absolute_value=True),
                        gu_all + ["a1r"], ["a1"], tag=f"am3_{tb}_{ti}")
                    dve(lambda e: e.tensor_single_scalar(a1[:], a1[:], 1e-5, op=AL.max),
                        ["a1"], ["a1"])
                    dve(lambda e, t=t: e.tensor_scalar(dsc[:, t:t + 1], a1[:],
                                                       swq[:, 2:3], 1.0 / 127.0,
                                                       op0=AL.mult, op1=AL.mult),
                        ["a1", "swq"], ["dsc"], tag=f"dsc{tb}_{ti}")
                    dve(lambda e: e.reciprocal(a2[:], a1[:]), ["a1", "a2r"], ["a2"])
                    pr.last_writer["a1r"] = pr.ops[-1]["idx"]
                    dve(lambda e: e.tensor_single_scalar(a2[:], a2[:], 127.0, op=AL.mult),
                        ["a2"], ["a2"])
                    dve(lambda e, ti=ti: e.tensor_scalar(gu[ti][:], gu[ti][:], a2[:], 0.0,
                                                         op0=AL.mult, op1=AL.add),
                        ["a2"] + gu_all, [f"guv{ti}"], tag=f"guv{tb}_{ti}")
                    pr.last_writer["a2r"] = pr.ops[-1]["idx"]
                    dve(lambda e, ti=ti: e.tensor_scalar(kq3[:], gu[ti][:], magicC[:],
                                                         magicC[:], op0=AL.add,
                                                         op1=AL.subtract),
                        [f"guv{ti}", "magicC", "kq3r"], ["kq3", f"gur{ti}"],
                        tag=f"round3_{tb}_{ti}")
                    for half in range(2):
                        def t3mm(e, half=half):
                            last = None
                            for c in range(16):
                                last = e.transpose(
                                    t_ps8[:, c * 128:(c + 1) * 128],
                                    kq3[:, (half * 16 + c) * 128:
                                         (half * 16 + c + 1) * 128], identb[:])
                            return last
                        pe(t3mm, ["kq3", "identb", "t_ps8"], ["t_ps8"],
                           tag=f"kq3T{tb}_{ti}_{half}")
                        dve(lambda e, ti=ti, half=half: e.tensor_copy(
                            kq3T[ti][:, half * 2048:(half + 1) * 2048],
                            t_ps8[:, 0:2048]),
                            ["t_ps8", f"kq3T{ti}"], [f"kq3T{ti}", "t_ps8"],
                            tag=f"kq3Tev{tb}_{ti}_{half}")
                    pr.last_writer["kq3r"] = pr.ops[-1]["idx"]
                for oh in range(2):
                    for cq in range(4):
                        wp = cq % 2
                        dma("sync" if wp == 0 else "gpsimd",
                            lambda e, oh=oh, cq=cq, wp=wp: e.dma_start(
                                out=wds[wp][:].rearrange("p (c w) -> p c w", c=8),
                                in_=d_wq[2][cq * 8:(cq + 1) * 8, :,
                                            oh * 512:(oh + 1) * 512]
                                .rearrange("c p w -> p c w")),
                            reads=[f"wds{wp}", "d_wdq"], writes=[f"wds{wp}"],
                            tag=f"ldwd{tb}_{oh}_{cq}")
                        for ti in range(TB):
                            def dmm(e, ti=ti, oh=oh, cq=cq, wp=wp):
                                last = None
                                for c in range(8):
                                    last = e.matmul(
                                        dn_ps[:, (ti * 2 + oh) * 512:
                                              (ti * 2 + oh + 1) * 512],
                                        kq3T[ti][:, (cq * 8 + c) * 128:
                                                  (cq * 8 + c + 1) * 128],
                                        wds[wp][:, c * 512:(c + 1) * 512],
                                        start=(cq == 0 and c == 0),
                                        stop=(cq == 3 and c == 7))
                                return last
                            pe(dmm, [f"kq3T{ti}", f"wds{wp}", "dn_ps"],
                               ["dn_ps", f"wds{wp}"],
                               tag=f"dmm{tb}_{ti}_{oh}_{cq}")
                    for ti in range(TB):
                        t = tb * TB + ti
                        dve(lambda e, ti=ti, oh=oh, t=t: e.tensor_scalar(
                            yt[ti][:, oh * 512:(oh + 1) * 512],
                            dn_ps[:, (ti * 2 + oh) * 512:(ti * 2 + oh + 1) * 512],
                            dsc[:, t:t + 1], 0.0, op0=AL.mult, op1=AL.add),
                            ["dn_ps", "dsc", f"yt{ti}_{oh}"], [f"yt{ti}_{oh}"],
                            tag=f"y1_{tb}_{ti}_{oh}")
                for ti in range(TB):
                    t = tb * TB + ti
                    dma("sync", lambda e, t=t: e.dma_start(
                        out=xp8[:], in_=d_xp[t * 128:(t + 1) * 128, :]),
                        reads=["xp8r", "d_xp"], writes=["xp8"], tag=f"ldxp8{t}")
                    dve(lambda e, ti=ti: e.tensor_tensor(yt[ti][:], yt[ti][:], gb_bc[:],
                                                         op=AL.mult),
                        [f"yt{ti}_0", f"yt{ti}_1", "gb_bc"], [f"yt{ti}"],
                        tag=f"y2_{tb}_{ti}")
                    dve(lambda e, ti=ti: e.tensor_tensor(yt[ti][:], yt[ti][:], xp8[:],
                                                         op=AL.add),
                        [f"yt{ti}", "xp8"], [f"yt{ti}"], tag=f"y3_{tb}_{ti}")
                    pr.last_writer["xp8r"] = pr.ops[-1]["idx"]
                    dma("scalar", lambda e, t=t, ti=ti: e.dma_start(
                        out=d_y[t * 128:(t + 1) * 128, :], in_=yt[ti][:]),
                        reads=[f"yt{ti}"], writes=["d_y"], tag=f"sty{t}")
                    pr.last_writer[f"yt{ti}_0"] = pr.ops[-1]["idx"]
                    pr.last_writer[f"yt{ti}_1"] = pr.ops[-1]["idx"]

        counters = pr.finalize()
        lane_names = list(ENGINES) + [f"{q}_d{i}" for q in ENGINES
                                      for i in range(Prog.NLANES)
                                      if any(l == f"{q}_d{i}" for l in counters)]
        with contextlib.ExitStack() as semstack:
            sems = {n: semstack.enter_context(nc.semaphore("sem_" + n))
                    for n in sorted(set(lane_names) | set(counters))}
            with nc.Block() as block:
                @block.sync
                def _(eng):
                    pr.emit_engine("sync", eng, sems)

                @block.vector
                def _(eng):
                    pr.emit_engine("vector", eng, sems)

                @block.scalar
                def _(eng):
                    pr.emit_engine("scalar", eng, sems)

                @block.tensor
                def _(eng):
                    pr.emit_engine("tensor", eng, sems)

                @block.gpsimd
                def _(eng):
                    pr.emit_engine("gpsimd", eng, sems)
                    for ln, tot in sorted(counters.items()):
                        if "_d" in ln and tot:
                            eng.wait_ge(sems[ln], tot)
    return nc


_NC_CACHE = {}


def _get_program():
    if "nc" not in _NC_CACHE:
        _NC_CACHE["nc"] = build_program()
    return _NC_CACHE["nc"]


def kernel(L0_attn_x, L0_attn_w_att_val, L0_attn_wv_w, L0_attn_wv_b, L0_attn_out_w,
           L0_attn_out_b, L0_ffn_w_gate, L0_ffn_w_up, L0_ffn_w_down, L0_c, L0_wmod):
    x = np.ascontiguousarray(np.asarray(L0_attn_x, np.float32))
    wav = np.asarray(L0_attn_w_att_val, np.float32)
    wavbd = np.zeros((NCH, 128, 128), np.float32)
    for c in range(NCH):
        wavbd[c, 0:64, 0:64] = wav[2 * c]
        wavbd[c, 64:128, 64:128] = wav[2 * c + 1]
    common = {
        "wav": wavbd,
        "wvT": np.ascontiguousarray(np.asarray(L0_attn_wv_w, np.float32).T),
        "wvb": np.asarray(L0_attn_wv_b, np.float32).reshape(1, E),
        "owT": np.ascontiguousarray(np.asarray(L0_attn_out_w, np.float32).T),
        "owb": np.asarray(L0_attn_out_b, np.float32).reshape(1, E),
        "wgT": np.ascontiguousarray(np.asarray(L0_ffn_w_gate, np.float32).T),
        "wuT": np.ascontiguousarray(np.asarray(L0_ffn_w_up, np.float32).T),
        "wdT": np.ascontiguousarray(np.asarray(L0_ffn_w_down, np.float32).T),
        "wmod": np.ascontiguousarray(np.asarray(L0_wmod, np.float32)),
        "ones128": np.ones((128, 128), np.float32),
        "onesA": np.concatenate([np.ones((64, 128), np.float32),
                                 np.zeros((64, 128), np.float32)], axis=0),
        "onesB": np.concatenate([np.zeros((64, 128), np.float32),
                                 np.ones((64, 128), np.float32)], axis=0),
        "ident128": np.eye(128, dtype=np.float32),
    }
    cmat = np.asarray(L0_c, np.float32)
    in_maps = []
    for core in range(8):
        b, half = core // 2, core % 2
        xb = x[b]
        roll = np.roll(np.arange(S), -half * TQ)
        im = dict(common)
        im["xT"] = np.ascontiguousarray(xb[roll].T)
        im["xo"] = np.ascontiguousarray(xb[half * TQ:(half + 1) * TQ])
        im["cc"] = np.ascontiguousarray(cmat[b].reshape(NCH, 128).T)
        in_maps.append(im)

    nc = _get_program()
    res = run_bass_kernel_spmd(nc, in_maps, core_ids=list(range(8)))
    out = np.empty((B, S, E), np.float32)
    for core in range(8):
        b, half = core // 2, core % 2
        out[b, half * TQ:(half + 1) * TQ] = res.results[core]["y"]
    return out
